# version 22
# baseline (speedup 1.0000x reference)
import contextlib
import numpy as np
import concourse.bass as bass
import concourse.mybir as mybir
from concourse.bass_utils import run_bass_kernel_spmd

F32 = mybir.dt.float32
BF16 = mybir.dt.bfloat16
AF = mybir.ActivationFunctionType
ALU = mybir.AluOpType
AX = mybir.AxisListType

P = 128
T = 2048
NH = 16
NB = 16
NCK = 16
NSB = 4
SBW = 512
EPS_RMS = 1e-6
EPS_LN = 1e-5
ENGS = ("sy", "gp", "te", "sc", "ve")


SERIALIZE = False


class Sched:

    def __init__(self):
        self.serial = False
        self.ops = {e: [] for e in ENGS}
        self.cnt = {}
        self.colls_seen = []
        self.wr = {}
        self.rd = {}

    def _merge(self, ws, tok):
        if tok is None:
            return
        e2, v2 = tok
        lim = 1 if e2.startswith("coll") else self.cnt.get(e2, 0)
        assert v2 <= lim, f"future token {tok}"
        ws[e2] = max(ws.get(e2, 0), v2)

    def op(self, eng, fn, waits=(), inc=1, coll=None, reads=(), writes=(),
           sem=None):
        ws = {}
        for w in waits:
            self._merge(ws, w)
        for (t, lo, hi) in reads:
            for s in self.wr.get(t, ()):
                if s[0] < hi and lo < s[1]:
                    self._merge(ws, s[2])
        for (t, lo, hi) in writes:
            for s in self.wr.get(t, ()):
                if s[0] < hi and lo < s[1]:
                    self._merge(ws, s[2])
            new_rd = []
            for r in self.rd.get(t, ()):
                if r[0] < hi and lo < r[1]:
                    for it in r[2].items():
                        self._merge(ws, it)
                    if r[0] < lo:
                        new_rd.append([r[0], lo, dict(r[2])])
                    if hi < r[1]:
                        new_rd.append([hi, r[1], dict(r[2])])
                else:
                    new_rd.append(r)
            self.rd[t] = new_rd
        if SERIALIZE or self.serial:
            for e2, v2 in self.cnt.items():
                if v2 > 0:
                    ws[e2] = v2
            for cname in self.colls_seen:
                ws[cname] = 1
        sem = sem or eng
        if coll is not None:
            tok = (coll, 1)
            self.colls_seen.append(coll)
        else:
            self.cnt[sem] = self.cnt.get(sem, 0) + inc
            tok = (sem, self.cnt[sem])
        for (t, lo, hi) in reads:
            rl = self.rd.setdefault(t, [])
            if rl and rl[-1][0] == lo and rl[-1][1] == hi:
                rl[-1][2][tok[0]] = max(rl[-1][2].get(tok[0], 0), tok[1])
            else:
                rl.append([lo, hi, {tok[0]: tok[1]}])
        for (t, lo, hi) in writes:
            segs = self.wr.setdefault(t, [])
            out = []
            for s in segs:
                if s[0] < hi and lo < s[1]:
                    if s[0] < lo:
                        out.append([s[0], lo, s[2]])
                    if hi < s[1]:
                        out.append([hi, s[1], s[2]])
                else:
                    out.append(s)
            out.append([lo, hi, tok])
            self.wr[t] = out
        self.ops[eng].append((fn, sorted(ws.items()), inc, coll, tok[0]))
        return tok


def build_nc():
    nc = bass.Bass("TRN2", target_bir_lowering=False)
    dt = nc.dram_tensor
    xT = dt("xT", [2048, T], F32, kind="ExternalInput").ap()
    qkvt = dt("qkvt", [48 * P, 2048], F32, kind="ExternalInput").ap()
    gatet = dt("gatet", [16 * P, 2048], F32, kind="ExternalInput").ap()
    woutt = dt("woutt", [16 * P, 2048], F32, kind="ExternalInput").ap()
    w3t = dt("w3t", [16 * P, 2048], F32, kind="ExternalInput").ap()
    swt = dt("swt", [8 * 4 * 2 * P, 4096], F32, kind="ExternalInput").ap()
    w1t = dt("w1t", [8 * P, 128], F32, kind="ExternalInput").ap()
    w2t = dt("w2t", [8 * P, 128], F32, kind="ExternalInput").ap()
    b1t = dt("b1t", [P, 8], F32, kind="ExternalInput").ap()
    b2t = dt("b2t", [P, 8], F32, kind="ExternalInput").ap()
    b3fm = dt("b3fm", [P, 16], F32, kind="ExternalInput").ap()
    nwfm = dt("nwfm", [P, 16], F32, kind="ExternalInput").ap()
    swbr = dt("swbr", [1, 8], F32, kind="ExternalInput").ap()
    slr = dt("slr", [1, 16], F32, kind="ExternalInput").ap()
    ohs = dt("ohs", [1, 4], F32, kind="ExternalInput").ap()
    ohr = dt("ohr", [1, 4], F32, kind="ExternalInput").ap()
    oha = dt("oha", [1, 4], F32, kind="ExternalInput").ap()
    yTo = dt("yTo", [2048, T], F32, kind="ExternalOutput").ap()
    out_sp = dt("out_sp", [2048, T], BF16, kind="Internal").ap()
    qx_sp = dt("qx_sp", [2048, T], BF16, kind="Internal").ap()
    y_sp = dt("y_sp", [2048, T], F32, kind="Internal").ap()
    cc_kv_i = dt("cc_kv_i", [4 * 2048, 128], BF16, kind="Internal").ap()
    cc_kv_o = dt("cc_kv_o", [4 * 2048, 128], BF16, kind="Internal").ap()
    cc_mk_i = dt("cc_mk_i", [4, 64], F32, kind="Internal").ap()
    cc_mk_o = dt("cc_mk_o", [4, 64], F32, kind="Internal").ap()
    cc_b1_o = dt("cc_b1_o", [4, 128], BF16, kind="Internal").ap()
    gate_bf = dt("gate_bf", [16 * P, 2048], BF16, kind="Internal").ap()
    wout_bf = dt("wout_bf", [16 * P, 2048], BF16, kind="Internal").ap()
    w3_bf = dt("w3_bf", [16 * P, 2048], BF16, kind="Internal").ap()
    cc_b2_o = dt("cc_b2_o", [4, 64], F32, kind="Internal").ap()

    es = contextlib.ExitStack()

    def sb(name, shape, dtype):
        return es.enter_context(nc.sbuf_tensor(name, shape, dtype))

    xgy = sb("xgy", [P, NCK * T], BF16)
    ar1 = sb("ar1", [P, 16384], BF16)
    ar2 = sb("ar2", [P, 16384], BF16)
    ysb = sb("ysb", [P, 8192], F32)
    ws = sb("ws", [P, 3 * 2048], BF16)
    swbuf = sb("swbuf", [P, 2 * 1024], F32)
    idxT = sb("idxT", [P, 128], F32)
    idxQ = sb("idxQ", [P, 128], F32)
    iotaJ = sb("iotaJ", [P, 16], F32)
    kIdx = sb("kIdx", [P, 1], F32)
    ident = sb("ident", [P, 128], BF16)
    decT = sb("decT", [P, 2 * 128], BF16)
    qdecT = sb("qdecT", [P, 2 * 128], BF16)
    prodf = sb("prodf", [P, 2 * 512], F32)
    accsc = sb("accsc", [1, 32], F32)
    kdecc = sb("kdecc", [P, 2], F32)
    dpow = sb("dpow", [P, 2 * 16], F32)
    slb = sb("slb", [P, 16], F32)
    nslb = sb("nslb", [P, 16], F32)
    s128 = sb("s128", [P, 16], F32)
    bdecb = sb("bdecb", [P, 16], F32)
    ohb = sb("ohb", [P, 12], F32)
    kvf = sb("kvf", [P, 128], F32)
    kvb = sb("kvb", [P, 128], BF16)
    kvsel = sb("kvsel", [P, 2048], BF16)
    onesb = sb("onesb", [P, 1], BF16)
    onesK = sb("onesK", [1, P], F32)
    onesM = sb("onesM", [P, 1], F32)
    nwsb = sb("nwsb", [P, 16], F32)
    b3sb = sb("b3sb", [P, 16], F32)
    b1sb = sb("b1sb", [P, 8], F32)
    b2sb = sb("b2sb", [P, 8], F32)
    w1b = sb("w1b", [P, 8 * 128], BF16)
    w2b = sb("w2b", [P, 8 * 128], BF16)
    w1s1 = sb("w1s1", [P, 128], BF16)
    w1s2 = sb("w1s2", [P, 128], BF16)
    w2s1 = sb("w2s1", [P, 128], BF16)
    w2s2 = sb("w2s2", [P, 128], BF16)
    b1s1 = sb("b1s1", [P, 1], F32)
    b1s2 = sb("b1s2", [P, 1], F32)
    b2c = sb("b2c", [P, 1], F32)
    qkTs = sb("qkTs", [P, 2 * 128], BF16)
    qps = sb("qps", [P, 2 * 128], BF16)
    outsq = sb("outsq", [P, 2 * 512], BF16)
    go1 = sb("go1", [P, 2 * 512], BF16)
    lnb = sb("lnb", [P, 2 * 512], BF16)
    sqb = sb("sqb", [P, 2 * 512], BF16)
    h1b = go1[:, 0:512]
    h2b = go1[:, 512:1024]
    accR = sb("accR", [P, 64], F32)
    accE = sb("accE", [P, 8], F32)
    rb = sb("rb", [P, 64], F32)
    slrs = sb("slrs", [1, 16], F32)
    swbs = sb("swbs", [1, 8], F32)
    ohss = sb("ohss", [1, 4], F32)
    ohrs = sb("ohrs", [1, 4], F32)
    ohas = sb("ohas", [1, 4], F32)
    ohrow = sb("ohrow", [1, 12], F32)
    rowsA = swbuf[0:1, 0:512]
    rowsB = swbuf[0:1, 512:1024]
    rowsC = swbuf[0:1, 1024:1536]
    rowsD = swbuf[0:1, 1536:2048]
    lgrow = sb("lgrow", [1, 8], F32)
    exrow = sb("exrow", [1, 8], F32)
    probs = sb("probs", [1, 8], F32)
    msk1 = sb("msk1", [1, 8], F32)
    msk2 = sb("msk2", [1, 8], F32)
    tmp8 = sb("tmp8", [1, 8], F32)
    m1r = sb("m1r", [1, 1], F32)
    m2r = sb("m2r", [1, 1], F32)
    srow = sb("srow", [1, 1], F32)
    pkrow = swbuf[0:1, 0:64]
    sendm = swbuf[0:1, 64:128]
    rmv = swbuf[0:1, 128:384]
    selrow = swbuf[0:1, 384:448]

    pMM = [es.enter_context(nc.psum_tensor(f"pMM{i}", [P, 512], F32)) for i in range(2)]
    pQK = [es.enter_context(nc.psum_tensor(f"pQK{i}", [P, 512], F32)) for i in range(2)]
    pO = [es.enter_context(nc.psum_tensor(f"pO{i}", [P, 512], F32)) for i in range(2)]
    pT = [es.enter_context(nc.psum_tensor(f"pT{i}", [P, 1024], BF16)) for i in range(2)]

    _sem_names = (list(ENGS) +
                  ["coll_kv", "coll_mk", "coll_b1", "coll_b2",
                   "dx", "dsm", "dw12", "dws0", "dws1", "dws2", "dsw0", "dsw1",
                   "doutw", "dqxw", "doutsb", "dqxsb", "dysb", "dywr",
                   "dcci", "drecv", "dmki", "drmv", "dyo"])
    sems = {n: es.enter_context(nc.semaphore(f"sem_{n}")) for n in _sem_names}

    S = Sched()

    A_qf = (0, 2048)
    A_kf = (2048, 4096)
    A_ktm = (4096, 6144)
    A_vt = (6144, 8192)
    A_qx = (8192, 10240)
    A_outb = (10240, 12288)
    A_kvs = (12288, 14336)
    B_outsb = (0, 8192)
    B_gateb = (0, 8192)
    B_go2b = (8192, 16384)
    C_junk = (0, 8192)

    def a1(rng, j0=0, w=None):
        a, b = rng
        return ar1[:, a:b] if w is None else ar1[:, a + j0:a + j0 + w]

    def a2(rng, j0=0, w=None):
        a, b = rng
        return ar2[:, a:b] if w is None else ar2[:, a + j0:a + j0 + w]

    def fa1(rng, j0=0, w=None):
        a, b = rng
        return ("ar1", a + j0, b if w is None else a + j0 + w)

    def fa2(rng, j0=0, w=None):
        a, b = rng
        return ("ar2", a + j0, b if w is None else a + j0 + w)

    SW0, SW1 = (0, 1024), (1024, 2048)
    R_A, R_B, R_C, R_D = (0, 512), (512, 1024), (1024, 1536), (1536, 2048)
    PKR, SDM, RMR, SELR = (0, 64), (64, 128), (128, 384), (384, 448)

    def fsw(rng):
        return ("swbuf", rng[0], rng[1])

    XGY_ALL = ("xgy", 0, NCK * T)

    def _preload(s):
        ins = None
        for f in (AF.Silu, AF.Sigmoid, AF.Exp, AF.Gelu, AF.Sqrt, AF.Square,
                  AF.Identity):
            ins = s.activation(swbuf[0:1, 1536:1544], swbuf[0:1, 1544:1552], f)
        return ins
    t_pre = S.op("sc", _preload, writes=[("acttab", 0, 1), ("swbuf", 1536, 1544)],
                 reads=[("swbuf", 1544, 1552)])

    t_gp_iota = S.op("gp", lambda g: (
        g.iota(idxT[:, :], [[1, 128]], base=0, channel_multiplier=-1,
               allow_small_or_imprecise_dtypes=True),
        g.affine_select(idxT[:, :], idxT[:, :], [[1, 128]], ALU.is_ge, 1e30,
                        base=0, channel_multiplier=-1),
        g.iota(idxQ[:, :], [[1, 128]], base=1, channel_multiplier=0,
               allow_small_or_imprecise_dtypes=True),
        g.iota(iotaJ[:, :], [[1, 16]], base=0, channel_multiplier=0,
               allow_small_or_imprecise_dtypes=True),
        g.iota(kIdx[:, :], [[1, 1]], base=127, channel_multiplier=-1,
               allow_small_or_imprecise_dtypes=True),
        g.memset(ident[:, :], 1.0),
        g.affine_select(ident[:, :], ident[:, :], [[1, 128]], ALU.is_equal, 0.0,
                        base=0, channel_multiplier=-1),
        g.memset(onesb[:, :], 1.0),
        g.memset(onesK[:, :], 1.0),
        g.memset(onesM[:, :], 1.0),
    )[-1], writes=[("idxT", 0, 128), ("idxQ", 0, 128), ("iotaJ", 0, 16),
                   ("kIdx", 0, 1), ("ident", 0, 128), ("onesb", 0, 1),
                   ("onesK", 0, 128), ("onesM", 0, 1)])

    for i in range(4):
        S.op("gp", lambda g, i=i: g.dma_start(
            out=xgy.rearrange("p (c t) -> p c t", t=T)[:, 4 * i:4 * i + 4, :],
            in_=xT.rearrange("(c p) t -> p c t", p=P)[:, 4 * i:4 * i + 4, :]),
            inc=16, sem="dx", waits=[t_pre],
            writes=[("xgy", i * 4 * T, (i + 1) * 4 * T)])

    _small_loads = [
        (slrs, slr, "slrs", 16), (nwsb, nwfm, "nwsb", 16), (b3sb, b3fm, "b3sb", 16),
        (swbs, swbr, "swbs", 8), (b1sb, b1t, "b1sb", 8), (b2sb, b2t, "b2sb", 8),
        (ohss, ohs, "ohss", 4), (ohrs, ohr, "ohrs", 4), (ohas, oha, "ohas", 4)]
    for _dst, _src, _k, _n in _small_loads:
        S.op("sy", lambda s, d=_dst, i=_src: s.dma_start(out=d[:, :], in_=i),
             inc=16, sem="dsm", waits=[t_pre], writes=[(_k, 0, _n)])

    S.op("gp", lambda g: g.dma_start(
        out=w1b.rearrange("p (e f) -> p e f", e=8),
        in_=w1t.rearrange("(e d) f -> d e f", d=P)), inc=16, sem="dw12",
        waits=[t_pre], writes=[("w1b", 0, 1024)])
    S.op("gp", lambda g: g.dma_start(
        out=w2b.rearrange("p (e f) -> p e f", e=8),
        in_=w2t.rearrange("(e d) f -> d e f", d=P)), inc=16, sem="dw12",
        writes=[("w2b", 0, 1024)])

    S.op("te", lambda t: t.matmul(pMM[0][:, 0:16], onesK[0:1, :],
                                  slrs[0:1, :], start=True, stop=True),
         reads=[("onesK", 0, 128), ("slrs", 0, 16)], writes=[("pMM0", 0, 512)])
    S.op("ve", lambda v: v.tensor_copy(slb[:, :], pMM[0][:, 0:16]),
         reads=[("pMM0", 0, 512)], writes=[("slb", 0, 16)])
    S.op("ve", lambda v: v.tensor_scalar(nslb[:, :], slb[:, :], -1.0, None,
                                         ALU.mult),
         reads=[("slb", 0, 16)], writes=[("nslb", 0, 16)])
    S.op("ve", lambda v: v.tensor_scalar(s128[:, :], slb[:, :], -128.0, None,
                                         ALU.mult),
         reads=[("slb", 0, 16)], writes=[("s128", 0, 16)])
    S.op("sc", lambda s: s.activation(bdecb[:, :], nslb[:, :], AF.Exp, scale=128.0),
         reads=[("nslb", 0, 16)], writes=[("bdecb", 0, 16)])
    S.op("ve", lambda v: (
        v.tensor_copy(ohrow[:, 0:4], ohss[:, :]),
        v.tensor_copy(ohrow[:, 4:8], ohrs[:, :]),
        v.tensor_copy(ohrow[:, 8:12], ohas[:, :]),
    )[-1], reads=[("ohss", 0, 4), ("ohrs", 0, 4), ("ohas", 0, 4)],
        writes=[("ohrow", 0, 12)])
    S.op("te", lambda t: t.matmul(pMM[1][:, 0:12], onesK[0:1, :],
                                  ohrow[0:1, :], start=True, stop=True),
         reads=[("onesK", 0, 128), ("ohrow", 0, 12)], writes=[("pMM1", 0, 512)])
    S.op("ve", lambda v: v.tensor_copy(ohb[:, :], pMM[1][:, 0:12]),
         reads=[("pMM1", 0, 512)], writes=[("ohb", 0, 12)])

    SER_A1 = SER_A2 = SER_A3 = SER_A4 = False
    for h in range(NH):
        sl = h % 2
        sq, sk, sv = (0, 1, 2) if h % 2 == 0 else (2, 0, 1)
        S.serial = SER_A4
        S.op("gp", lambda g, h=h, sq=sq: g.dma_start(
            out=ws[:, sq * 2048:(sq + 1) * 2048],
            in_=qkvt[(3 * h) * P:(3 * h + 1) * P, :]), inc=16, sem=f"dws{sq}",
            waits=[t_pre if h == 0 else None],
            writes=[("ws", sq * 2048, (sq + 1) * 2048)])
        S.op("gp", lambda g, h=h, sk=sk: g.dma_start(
            out=ws[:, sk * 2048:(sk + 1) * 2048],
            in_=qkvt[(3 * h + 1) * P:(3 * h + 2) * P, :]), inc=16, sem=f"dws{sk}",
            writes=[("ws", sk * 2048, (sk + 1) * 2048)])
        S.op("gp", lambda g, h=h, sv=sv: g.dma_start(
            out=ws[:, sv * 2048:(sv + 1) * 2048],
            in_=qkvt[(3 * h + 2) * P:(3 * h + 3) * P, :]), inc=16, sem=f"dws{sv}",
            writes=[("ws", sv * 2048, (sv + 1) * 2048)])

        S.op("sc", lambda s, h=h, sl=sl: (
            s.activation(decT[:, sl * 128:(sl + 1) * 128], idxT[:, :], AF.Exp,
                         scale=nslb[:, h:h + 1]),
            s.activation(qdecT[:, sl * 128:(sl + 1) * 128], idxQ[:, :], AF.Exp,
                         scale=nslb[:, h:h + 1]),
            s.activation(kdecc[:, sl:sl + 1], kIdx[:, :], AF.Exp,
                         scale=nslb[:, h:h + 1]),
            s.activation(dpow[:, sl * 16:(sl + 1) * 16], iotaJ[:, :], AF.Exp,
                         scale=s128[:, h:h + 1]),
        )[-1],
            reads=[("idxT", 0, 128), ("idxQ", 0, 128), ("kIdx", 0, 1),
                   ("iotaJ", 0, 16), ("nslb", 0, 16), ("s128", 0, 16)],
            writes=[("decT", sl * 128, (sl + 1) * 128),
                    ("qdecT", sl * 128, (sl + 1) * 128),
                    ("kdecc", sl, sl + 1), ("dpow", sl * 16, (sl + 1) * 16)])
        S.serial = SER_A1
        for (slot, dstrng) in ((sq, A_qf), (sk, A_kf)):
            for isb in range(NSB):
                par = isb % 2

                def mmgrp(t, slot=slot, isb=isb, par=par):
                    ins = None
                    for fc in range(NCK):
                        ins = t.matmul(
                            pMM[par][:, :],
                            ws[:, slot * 2048 + fc * 128:slot * 2048 + (fc + 1) * 128],
                            xgy[:, fc * T + isb * SBW: fc * T + (isb + 1) * SBW],
                            start=(fc == 0), stop=(fc == NCK - 1))
                    return ins
                S.op("te", mmgrp,
                     reads=[("ws", slot * 2048, (slot + 1) * 2048), XGY_ALL],
                     writes=[(f"pMM{par}", 0, 512)])
                S.op("sc", lambda s, isb=isb, par=par, dstrng=dstrng: s.activation(
                    a1(dstrng, isb * SBW, SBW), pMM[par][:, :], AF.Silu),
                    reads=[(f"pMM{par}", 0, 512)],
                    writes=[fa1(dstrng, isb * SBW, SBW)])

        S.serial = SER_A1
        for tc in range(NB):
            par = tc % 2

            def mmv(t, tc=tc, par=par, slot=sv):
                ins = None
                for fc in range(NCK):
                    ins = t.matmul(
                        pMM[par][:, 0:128],
                        xgy[:, fc * T + tc * 128: fc * T + (tc + 1) * 128],
                        ws[:, slot * 2048 + fc * 128:slot * 2048 + (fc + 1) * 128],
                        start=(fc == 0), stop=(fc == NCK - 1))
                return ins
            S.op("te", mmv,
                 reads=[("ws", sv * 2048, (sv + 1) * 2048), XGY_ALL],
                 writes=[(f"pMM{par}", 0, 512)])
            S.op("sc", lambda s, tc=tc, par=par: s.activation(
                a1(A_vt, tc * 128, 128), pMM[par][:, 0:128], AF.Silu),
                reads=[(f"pMM{par}", 0, 512)], writes=[fa1(A_vt, tc * 128, 128)])

        S.serial = SER_A2
        for j in range(NB):
            par = j % 2
            S.op("te", lambda t, j=j, par=par: t.transpose(
                pT[par][:, 0:128], a1(A_kf, j * 128, 128), ident[:, :]),
                reads=[fa1(A_kf, j * 128, 128), ("ident", 0, 128)],
                writes=[(f"pT{par}", 0, 1024)])
            S.op("ve", lambda v, j=j, par=par, sl=sl: v.tensor_scalar(
                a1(A_ktm, j * 128, 128), pT[par][:, 0:128], kdecc[:, sl:sl + 1],
                None, ALU.mult),
                reads=[(f"pT{par}", 0, 1024), ("kdecc", sl, sl + 1)],
                writes=[fa1(A_ktm, j * 128, 128)])

        S.serial = SER_A4
        S.op("gp", lambda g: g.memset(kvf[:, :], 0.0), writes=[("kvf", 0, 128)])
        S.op("ve", lambda v: v.memset(kvb[:, :], 0.0), writes=[("kvb", 0, 128)])

        S.serial = SER_A3

        def mm1_op(j):
            par = j % 2
            S.op("te", lambda t, j=j, par=par: t.matmul(
                pQK[par][:, 0:128], a1(A_kf, j * 128, 128), a1(A_qf, j * 128, 128),
                start=True, stop=True),
                reads=[fa1(A_kf, j * 128, 128), fa1(A_qf, j * 128, 128)],
                writes=[(f"pQK{par}", 0, 512)])

        mm1_op(0)
        for j in range(NB):
            par = j % 2
            S.op("ve", lambda v, par=par, sl=sl: v.tensor_tensor(
                qkTs[:, par * 128:(par + 1) * 128], pQK[par][:, 0:128],
                decT[:, sl * 128:(sl + 1) * 128], ALU.mult),
                reads=[(f"pQK{par}", 0, 512), ("decT", sl * 128, (sl + 1) * 128)],
                writes=[("qkTs", par * 128, (par + 1) * 128)])
            S.op("ve", lambda g, j=j, par=par, sl=sl: g.tensor_tensor(
                qps[:, par * 128:(par + 1) * 128], a1(A_qf, j * 128, 128),
                qdecT[:, sl * 128:(sl + 1) * 128], ALU.mult),
                reads=[fa1(A_qf, j * 128, 128), ("qdecT", sl * 128, (sl + 1) * 128)],
                writes=[("qps", par * 128, (par + 1) * 128)])
            S.op("ve", lambda g, j=j, par=par, sl=sl: g.tensor_scalar(
                a1(A_qx, j * 128, 128), qps[:, par * 128:(par + 1) * 128],
                dpow[:, sl * 16 + j:sl * 16 + j + 1], None, ALU.mult),
                reads=[("qps", par * 128, (par + 1) * 128),
                       ("dpow", sl * 16 + j, sl * 16 + j + 1)],
                writes=[fa1(A_qx, j * 128, 128)])
            if j + 1 < NB:
                mm1_op(j + 1)
            S.op("te", lambda t, j=j, par=par: t.matmul(
                pO[par][:, 0:128], a1(A_vt, j * 128, 128),
                qkTs[:, par * 128:(par + 1) * 128], start=True, stop=False),
                reads=[fa1(A_vt, j * 128, 128), ("qkTs", par * 128, (par + 1) * 128)],
                writes=[(f"pO{par}", 0, 512)])
            S.op("te", lambda t, par=par: t.matmul(
                pO[par][:, 0:128], kvb[:, :], qps[:, par * 128:(par + 1) * 128],
                start=False, stop=True),
                reads=[("kvb", 0, 128), ("qps", par * 128, (par + 1) * 128)],
                writes=[(f"pO{par}", 0, 512)])
            S.op("sc", lambda s, j=j, par=par: s.copy(
                a1(A_outb, j * 128, 128), pO[par][:, 0:128]),
                reads=[(f"pO{par}", 0, 512)], writes=[fa1(A_outb, j * 128, 128)])
            S.op("te", lambda t, j=j, par=par: t.matmul(
                pMM[par][:, 0:128], a1(A_ktm, j * 128, 128), a1(A_vt, j * 128, 128),
                start=True, stop=True),
                reads=[fa1(A_ktm, j * 128, 128), fa1(A_vt, j * 128, 128)],
                writes=[(f"pMM{par}", 0, 512)])
            S.op("ve", lambda v, h=h, par=par: v.scalar_tensor_tensor(
                kvf[:, :], kvf[:, :], bdecb[:, h:h + 1], pMM[par][:, 0:128],
                ALU.mult, ALU.add),
                reads=[("kvf", 0, 128), ("bdecb", 0, 16), (f"pMM{par}", 0, 512)],
                writes=[("kvf", 0, 128)])
            S.op("ve", lambda g: g.tensor_copy(kvb[:, :], kvf[:, :]),
                 reads=[("kvf", 0, 128)], writes=[("kvb", 0, 128)])

        S.serial = SER_A4
        S.op("sy", lambda s, h=h: s.dma_start(
            out=out_sp[h * P:(h + 1) * P, :], in_=a1(A_outb)), inc=16, sem="doutw",
            reads=[fa1(A_outb)], writes=[("out_sp", 0, 1)])
        S.op("sc", lambda s, h=h: s.dma_start(
            out=qx_sp[h * P:(h + 1) * P, :], in_=a1(A_qx)), inc=16, sem="dqxw",
            reads=[fa1(A_qx)], writes=[("qx_sp", 0, 1)])
        S.op("ve", lambda v, h=h: v.tensor_copy(
            a1(A_kvs, h * 128, 128), kvf[:, :]),
            reads=[("kvf", 0, 128)], writes=[fa1(A_kvs, h * 128, 128)])

    S.serial = False
    for s4 in range(4):
        S.op("ve", lambda v, s4=s4: v.tensor_scalar(
            a1(A_qx), a1(A_kvs), ohb[:, s4:s4 + 1], None, ALU.mult),
            reads=[fa1(A_kvs), ("ohb", s4, s4 + 1)], writes=[fa1(A_qx)])
        S.op("gp", lambda g, s4=s4: g.dma_start(
            out=cc_kv_i[s4 * 2048:(s4 + 1) * 2048, :].rearrange("(h d) e -> d h e", d=P),
            in_=a1(A_qx).rearrange("p (h e) -> p h e", e=128)), inc=16, sem="dcci",
            reads=[fa1(A_qx)], writes=[("cc_kv_i", s4, s4 + 1)])
    t_cc_kv = S.op("gp", lambda g: g.collective_compute(
        "AllReduce", ALU.add, replica_groups=[list(range(8))],
        ins=[cc_kv_i], outs=[cc_kv_o]), coll="coll_kv",
        reads=[("cc_kv_i", 0, 4)], writes=[("cc_kv_o", 0, 1)])
    t_cc_b1 = S.op("gp", lambda g: g.collective_compute(
        "AllReduce", ALU.add, replica_groups=[list(range(8))],
        ins=[cc_kv_o[0:4, :]], outs=[cc_b1_o]), coll="coll_b1",
        reads=[("cc_kv_o", 0, 1)], writes=[("cc_b1_o", 0, 1)])
    _prev_recv = [None]
    for s4 in range(4):
        _prev_recv[0] = S.op("gp", lambda g, s4=s4: g.dma_start(
            out=a1(A_qx, (s4 % 4) * 2048 - 8192 + 8192, 2048).rearrange(
                "p (h e) -> p h e", e=128) if False else
            ar1[:, 8192 + s4 * 2048:8192 + (s4 + 1) * 2048].rearrange(
                "p (h e) -> p h e", e=128),
            in_=cc_kv_o[s4 * 2048:(s4 + 1) * 2048, :].rearrange("(h d) e -> d h e", d=P)),
            inc=16, sem="drecv", waits=[t_cc_b1, _prev_recv[0]],
            reads=[("cc_kv_o", 0, 1)],
            writes=[("ar1", 8192 + s4 * 2048, 8192 + (s4 + 1) * 2048)])
    S.op("ve", lambda v: v.memset(kvsel[:, :], 0.0), writes=[("kvsel", 0, 2048)])
    for s4 in range(4):
        S.op("ve", lambda v, s4=s4: v.scalar_tensor_tensor(
            kvsel[:, :], ar1[:, 8192 + s4 * 2048:8192 + (s4 + 1) * 2048],
            ohb[:, 4 + s4:5 + s4],
            kvsel[:, :], ALU.mult, ALU.add),
            reads=[("ar1", 8192 + s4 * 2048, 8192 + (s4 + 1) * 2048),
                   ("ohb", 4 + s4, 5 + s4), ("kvsel", 0, 2048)],
            writes=[("kvsel", 0, 2048)])

    S.serial = False
    xgy_ybf_w = [None] * NSB
    for isb in range(NSB):
        S.op("sy", lambda s, isb=isb: s.dma_start(
            out=a1(B_outsb).rearrange("p (c t) -> p c t", t=SBW),
            in_=out_sp.rearrange("(c p) t -> p c t", p=P)[:, :, isb * SBW:(isb + 1) * SBW]),
            inc=16, sem="doutsb", reads=[("out_sp", 0, 1)], writes=[fa1(B_outsb)])
        S.op("sy", lambda s, isb=isb: s.dma_start(
            out=a2(B_go2b).rearrange("p (c t) -> p c t", t=SBW),
            in_=qx_sp.rearrange("(c p) t -> p c t", p=P)[:, :, isb * SBW:(isb + 1) * SBW]),
            inc=16, sem="dqxsb", reads=[("qx_sp", 0, 1)], writes=[fa2(B_go2b)])
        S.op("sc", lambda s, isb=isb: s.dma_start(
            out=ysb.rearrange("p (c t) -> p c t", t=SBW),
            in_=xT.rearrange("(c p) t -> p c t", p=P)[:, :, isb * SBW:(isb + 1) * SBW]),
            inc=16, sem="dysb", writes=[("ysb", 0, 8192)])
        for oc in range(NCK):
            slw = oc % 3
            par = oc % 2
            S.op("gp", lambda g, oc=oc, slw=slw: g.dma_start(
                out=ws[:, slw * 2048:(slw + 1) * 2048],
                in_=gatet[oc * P:(oc + 1) * P, :]), inc=16, sem=f"dws{slw}",
                writes=[("ws", slw * 2048, (slw + 1) * 2048)])

            def gmm(t, slw=slw, isb=isb, par=par):
                ins = None
                for fc in range(NCK):
                    ins = t.matmul(
                        pMM[par][:, :],
                        ws[:, slw * 2048 + fc * 128:slw * 2048 + (fc + 1) * 128],
                        xgy[:, fc * T + isb * SBW: fc * T + (isb + 1) * SBW],
                        start=(fc == 0), stop=(fc == NCK - 1))
                return ins
            S.op("te", gmm,
                 reads=[("ws", slw * 2048, (slw + 1) * 2048), XGY_ALL],
                 writes=[(f"pMM{par}", 0, 512)])
            S.op("sc", lambda s, oc=oc, par=par: s.activation(
                a2(B_gateb, oc * SBW, SBW), pMM[par][:, :], AF.Sigmoid),
                reads=[(f"pMM{par}", 0, 512)], writes=[fa2(B_gateb, oc * SBW, SBW)])

        for oc in range(NCK):
            par = oc % 2
            S.op("te", lambda t, oc=oc, par=par: t.matmul(
                pMM[par][:, :], kvsel[:, oc * 128:(oc + 1) * 128],
                a2(B_go2b, oc * SBW, SBW), start=True, stop=True),
                reads=[("kvsel", oc * 128, (oc + 1) * 128),
                       fa2(B_go2b, oc * SBW, SBW)],
                writes=[(f"pMM{par}", 0, 512)])
            S.op("ve", lambda v, oc=oc, par=par: v.tensor_tensor(
                a1(B_outsb, oc * SBW, SBW), a1(B_outsb, oc * SBW, SBW),
                pMM[par][:, :], ALU.add),
                reads=[fa1(B_outsb, oc * SBW, SBW), (f"pMM{par}", 0, 512)],
                writes=[fa1(B_outsb, oc * SBW, SBW)])

        for oc in range(NCK):
            par = oc % 2
            S.op("sc", lambda s, oc=oc, par=par: s.activation(
                outsq[:, par * 512:(par + 1) * 512], a1(B_outsb, oc * SBW, SBW),
                AF.Square),
                reads=[fa1(B_outsb, oc * SBW, SBW)],
                writes=[("outsq", par * 512, (par + 1) * 512)])
            S.op("te", lambda t, oc=oc, par=par: t.matmul(
                pQK[0][0:1, :], onesb[:, :], outsq[:, par * 512:(par + 1) * 512],
                start=(oc == 0), stop=(oc == NCK - 1)),
                reads=[("onesb", 0, 1), ("outsq", par * 512, (par + 1) * 512)],
                writes=[("pQK0", 0, 512)])
        S.op("ve", lambda v: v.tensor_scalar(
            rowsA, pQK[0][0:1, :], 1.0 / 2048.0, EPS_RMS, ALU.mult, ALU.add),
            reads=[("pQK0", 0, 512)], writes=[fsw(R_A)])
        S.op("sc", lambda s: s.activation(rowsB, rowsA, AF.Ln),
             reads=[fsw(R_A)], writes=[fsw(R_B)])
        S.op("sc", lambda s: s.activation(rowsB, rowsB, AF.Exp, scale=-0.5),
             reads=[fsw(R_B)], writes=[fsw(R_B)])
        S.op("te", lambda t: t.matmul(pQK[1][:, :], onesK[0:1, :], rowsB,
                                      start=True, stop=True),
             reads=[("onesK", 0, 128), fsw(R_B)], writes=[("pQK1", 0, 512)])
        for oc in range(NCK):
            par = oc % 2
            S.op("ve", lambda v, oc=oc, par=par: v.scalar_tensor_tensor(
                go1[:, par * 512:(par + 1) * 512], a2(B_gateb, oc * SBW, SBW),
                nwsb[:, oc:oc + 1], a1(B_outsb, oc * SBW, SBW), ALU.mult, ALU.mult),
                reads=[fa2(B_gateb, oc * SBW, SBW), ("nwsb", oc, oc + 1),
                       fa1(B_outsb, oc * SBW, SBW)],
                writes=[("go1", par * 512, (par + 1) * 512)])
            S.op("ve", lambda v, oc=oc, par=par: v.tensor_tensor(
                a2(B_go2b, oc * SBW, SBW), go1[:, par * 512:(par + 1) * 512],
                pQK[1][:, :], ALU.mult),
                reads=[("go1", par * 512, (par + 1) * 512), ("pQK1", 0, 512)],
                writes=[fa2(B_go2b, oc * SBW, SBW)])

        for gc in range(NCK):
            slw = gc % 3
            par = gc % 2
            S.op("gp", lambda g, gc=gc, slw=slw: g.dma_start(
                out=ws[:, slw * 2048:(slw + 1) * 2048],
                in_=woutt[gc * P:(gc + 1) * P, :]), inc=16, sem=f"dws{slw}",
                writes=[("ws", slw * 2048, (slw + 1) * 2048)])

            def wmm(t, slw=slw, par=par):
                ins = None
                for oc in range(NCK):
                    ins = t.matmul(
                        pMM[par][:, :],
                        ws[:, slw * 2048 + oc * 128:slw * 2048 + (oc + 1) * 128],
                        a2(B_go2b, oc * SBW, SBW),
                        start=(oc == 0), stop=(oc == NCK - 1))
                return ins
            S.op("te", wmm,
                 reads=[("ws", slw * 2048, (slw + 1) * 2048), fa2(B_go2b)],
                 writes=[(f"pMM{par}", 0, 512)])
            S.op("ve", lambda v, gc=gc, par=par: v.scalar_tensor_tensor(
                ysb[:, gc * SBW:(gc + 1) * SBW], pMM[par][:, :], 0.0,
                ysb[:, gc * SBW:(gc + 1) * SBW], ALU.add, ALU.add),
                reads=[(f"pMM{par}", 0, 512), ("ysb", gc * SBW, (gc + 1) * SBW)],
                writes=[("ysb", gc * SBW, (gc + 1) * SBW)])

        def ln_stats_norm(store_fn):
            for gc in range(NCK):
                par = gc % 2
                S.op("ve", lambda s, gc=gc, par=par: s.tensor_copy(
                    lnb[:, par * 512:(par + 1) * 512], ysb[:, gc * SBW:(gc + 1) * SBW]),
                    reads=[("ysb", gc * SBW, (gc + 1) * SBW)],
                    writes=[("lnb", par * 512, (par + 1) * 512)])
                S.op("te", lambda t, gc=gc, par=par: t.matmul(
                    pO[0][0:1, :], onesb[:, :], lnb[:, par * 512:(par + 1) * 512],
                    start=(gc == 0), stop=(gc == NCK - 1)),
                    reads=[("onesb", 0, 1), ("lnb", par * 512, (par + 1) * 512)],
                    writes=[("pO0", 0, 512)])
                S.op("sc", lambda s, gc=gc, par=par: s.activation(
                    sqb[:, par * 512:(par + 1) * 512], ysb[:, gc * SBW:(gc + 1) * SBW],
                    AF.Square),
                    reads=[("ysb", gc * SBW, (gc + 1) * SBW)],
                    writes=[("sqb", par * 512, (par + 1) * 512)])
                S.op("te", lambda t, gc=gc, par=par: t.matmul(
                    pO[1][0:1, :], onesb[:, :], sqb[:, par * 512:(par + 1) * 512],
                    start=(gc == 0), stop=(gc == NCK - 1)),
                    reads=[("onesb", 0, 1), ("sqb", par * 512, (par + 1) * 512)],
                    writes=[("pO1", 0, 512)])
            S.op("ve", lambda v: v.tensor_scalar(
                rowsC, pO[0][0:1, :], 1.0 / 2048.0, None, ALU.mult),
                reads=[("pO0", 0, 512)], writes=[fsw(R_C)])
            S.op("ve", lambda v: v.tensor_scalar(
                rowsA, pO[1][0:1, :], 1.0 / 2048.0, None, ALU.mult),
                reads=[("pO1", 0, 512)], writes=[fsw(R_A)])
            S.op("ve", lambda v: v.tensor_tensor(rowsD, rowsC, rowsC, ALU.mult),
                 reads=[fsw(R_C)], writes=[fsw(R_D)])
            S.op("ve", lambda v: v.tensor_sub(rowsA, rowsA, rowsD),
                 reads=[fsw(R_A), fsw(R_D)], writes=[fsw(R_A)])
            S.op("ve", lambda v: v.tensor_scalar(
                rowsA, rowsA, 1.0, EPS_LN, ALU.mult, ALU.add),
                reads=[fsw(R_A)], writes=[fsw(R_A)])
            S.op("sc", lambda s: s.activation(rowsB, rowsA, AF.Ln),
                 reads=[fsw(R_A)], writes=[fsw(R_B)])
            S.op("sc", lambda s: s.activation(rowsB, rowsB, AF.Exp, scale=-0.5),
                 reads=[fsw(R_B)], writes=[fsw(R_B)])
            S.op("ve", lambda v: v.tensor_scalar(rowsC, rowsC, -1.0, None, ALU.mult),
                 reads=[fsw(R_C)], writes=[fsw(R_C)])
            S.op("te", lambda t: t.matmul(pO[0][:, :], onesK[0:1, :], rowsC,
                                          start=True, stop=True),
                 reads=[("onesK", 0, 128), fsw(R_C)], writes=[("pO0", 0, 512)])
            S.op("te", lambda t: t.matmul(pO[1][:, :], onesK[0:1, :], rowsB,
                                          start=True, stop=True),
                 reads=[("onesK", 0, 128), fsw(R_B)], writes=[("pO1", 0, 512)])
            for gc in range(NCK):
                S.op("ve", lambda v, gc=gc: v.tensor_tensor(
                    ysb[:, gc * SBW:(gc + 1) * SBW], ysb[:, gc * SBW:(gc + 1) * SBW],
                    pO[0][:, :], ALU.add),
                    reads=[("ysb", gc * SBW, (gc + 1) * SBW), ("pO0", 0, 512)],
                    writes=[("ysb", gc * SBW, (gc + 1) * SBW)])
                S.op("ve", lambda v, gc=gc: v.tensor_tensor(
                    ysb[:, gc * SBW:(gc + 1) * SBW], ysb[:, gc * SBW:(gc + 1) * SBW],
                    pO[1][:, :], ALU.mult),
                    reads=[("ysb", gc * SBW, (gc + 1) * SBW), ("pO1", 0, 512)],
                    writes=[("ysb", gc * SBW, (gc + 1) * SBW)])
                store_fn(gc)

        def b_store(gc, isb=isb):
            xgy_ybf_w[isb] = S.op("ve", lambda v, gc=gc, isb=isb: v.tensor_copy(
                xgy[:, gc * T + isb * SBW: gc * T + (isb + 1) * SBW],
                ysb[:, gc * SBW:(gc + 1) * SBW]),
                reads=[("ysb", gc * SBW, (gc + 1) * SBW)],
                writes=[("xgy", gc * T + isb * SBW, gc * T + (isb + 1) * SBW)])
        ln_stats_norm(b_store)
        S.op("sc", lambda s, isb=isb: s.dma_start(
            out=y_sp.rearrange("(c p) t -> p c t", p=P)[:, :, isb * SBW:(isb + 1) * SBW],
            in_=ysb.rearrange("p (c t) -> p c t", t=SBW)), inc=16, sem="dywr",
            reads=[("ysb", 0, 8192)],
            writes=[("y_sp", isb * SBW, (isb + 1) * SBW)])

        for e in range(8):
            ebank = f"pQK{e % 2}"
            on_ve = True
            for half2 in range(8):
                fc0 = half2 * 2
                hf, q4 = fc0 // 8, (fc0 % 8) // 2
                slot = half2 % 2
                row0 = (e * 8 + isb * 2 + hf) * P
                qeng = "sy" if half2 % 2 == 0 else "sc"
                S.op(qeng, lambda g, slot=slot, row0=row0, q4=q4: g.dma_start(
                    out=swbuf[:, slot * 1024:(slot + 1) * 1024],
                    in_=swt[row0:row0 + P, q4 * 1024:(q4 + 1) * 1024]),
                    inc=16, sem=f"dsw{slot}",
                    writes=[("swbuf", slot * 1024, (slot + 1) * 1024)])
                for i2 in range(2):
                    fc = fc0 + i2
                    S.op("ve", lambda v, fc=fc, slot=slot, i2=i2: v.tensor_tensor(
                        prodf[:, (slot % 2) * 512:(slot % 2 + 1) * 512],
                        ysb[:, fc * 512:(fc + 1) * 512],
                        swbuf[:, slot * 1024 + i2 * 512:slot * 1024 + (i2 + 1) * 512],
                        ALU.mult),
                        reads=[("ysb", fc * 512, (fc + 1) * 512),
                               ("swbuf", slot * 1024 + i2 * 512,
                                slot * 1024 + (i2 + 1) * 512)],
                        writes=[("prodf", (slot % 2) * 512, (slot % 2 + 1) * 512)])
                    if on_ve:
                        if fc == 0:
                            S.op("ve", lambda v, slot=slot, e=e: v.tensor_copy(
                                pQK[e % 2][:, :],
                                prodf[:, (slot % 2) * 512:(slot % 2 + 1) * 512]),
                                reads=[("prodf", (slot % 2) * 512,
                                        (slot % 2 + 1) * 512)],
                                writes=[(ebank, 0, 512)])
                        else:
                            S.op("ve", lambda v, slot=slot, e=e: v.tensor_tensor(
                                pQK[e % 2][:, :], pQK[e % 2][:, :],
                                prodf[:, (slot % 2) * 512:(slot % 2 + 1) * 512],
                                ALU.add),
                                reads=[(ebank, 0, 512),
                                       ("prodf", (slot % 2) * 512,
                                        (slot % 2 + 1) * 512)],
                                writes=[(ebank, 0, 512)])
                    else:
                        S.op("te", lambda t, fc=fc, slot=slot, e=e: t.matmul(
                            pQK[e % 2][0:1, :], onesM[:, :],
                            prodf[:, (slot % 2) * 512:(slot % 2 + 1) * 512],
                            start=(fc == 0), stop=(fc == NCK - 1)),
                            reads=[("onesM", 0, 1),
                                   ("prodf", (slot % 2) * 512, (slot % 2 + 1) * 512)],
                            writes=[(ebank, 0, 512)])
            if on_ve:
                S.op("sc", lambda s, e=e: s.copy(
                    prodf[:, 0:512], pQK[e % 2][:, :]),
                    reads=[(ebank, 0, 512)], writes=[("prodf", 0, 512)])
                S.op("te", lambda t, e=e: t.matmul(
                    pQK[e % 2][0:1, :], onesM[:, :], prodf[:, 0:512],
                    start=True, stop=True),
                    reads=[("onesM", 0, 1), ("prodf", 0, 512)],
                    writes=[(ebank, 0, 512)])
            col = e * 4 + isb
            S.op("ve", lambda v, e=e, col=col: v.reduce_sum(
                accsc[0:1, col:col + 1], pQK[e % 2][0:1, :], axis=AX.X),
                reads=[(f"pQK{e % 2}", 0, 512)], writes=[("accsc", col, col + 1)])

    S.serial = False
    for e in range(8):
        S.op("ve", lambda v, e=e: v.reduce_sum(
            lgrow[:, e:e + 1], accsc[0:1, e * 4:(e + 1) * 4], axis=AX.X),
            reads=[("accsc", e * 4, (e + 1) * 4)], writes=[("lgrow", e, e + 1)])
    S.op("ve", lambda v: v.tensor_add(lgrow[:, :], lgrow[:, :], swbs[:, :]),
         reads=[("lgrow", 0, 8), ("swbs", 0, 8)], writes=[("lgrow", 0, 8)])
    S.op("ve", lambda v: v.reduce_max(m1r[:, :], lgrow[:, :], axis=AX.X),
         reads=[("lgrow", 0, 8)], writes=[("m1r", 0, 1)])
    S.op("ve", lambda v: v.tensor_scalar(m1r[:, :], m1r[:, :], -1.0, None, ALU.mult),
         reads=[("m1r", 0, 1)], writes=[("m1r", 0, 1)])
    S.op("sc", lambda s: s.activation(exrow[:, :], lgrow[:, :], AF.Exp,
                                      bias=m1r[0:1, 0:1]),
         reads=[("lgrow", 0, 8), ("m1r", 0, 1)], writes=[("exrow", 0, 8)])
    S.op("ve", lambda v: v.reduce_sum(srow[:, :], exrow[:, :], axis=AX.X),
         reads=[("exrow", 0, 8)], writes=[("srow", 0, 1)])
    S.op("ve", lambda v: v.reciprocal(srow[:, :], srow[:, :]),
         reads=[("srow", 0, 1)], writes=[("srow", 0, 1)])
    S.op("ve", lambda v: v.tensor_scalar(probs[:, :], exrow[:, :], srow[0:1, 0:1],
                                         None, ALU.mult),
         reads=[("exrow", 0, 8), ("srow", 0, 1)], writes=[("probs", 0, 8)])
    S.op("ve", lambda v: v.reduce_max(m1r[:, :], probs[:, :], axis=AX.X),
         reads=[("probs", 0, 8)], writes=[("m1r", 0, 1)])
    S.op("ve", lambda v: v.tensor_scalar(msk1[:, :], probs[:, :], m1r[0:1, 0:1],
                                         None, ALU.is_ge),
         reads=[("probs", 0, 8), ("m1r", 0, 1)], writes=[("msk1", 0, 8)])
    S.op("ve", lambda v: v.tensor_mul(tmp8[:, :], probs[:, :], msk1[:, :]),
         reads=[("probs", 0, 8), ("msk1", 0, 8)], writes=[("tmp8", 0, 8)])
    S.op("ve", lambda v: v.tensor_sub(tmp8[:, :], probs[:, :], tmp8[:, :]),
         reads=[("probs", 0, 8), ("tmp8", 0, 8)], writes=[("tmp8", 0, 8)])
    S.op("ve", lambda v: v.reduce_max(m2r[:, :], tmp8[:, :], axis=AX.X),
         reads=[("tmp8", 0, 8)], writes=[("m2r", 0, 1)])
    S.op("ve", lambda v: v.tensor_scalar(msk2[:, :], tmp8[:, :], m2r[0:1, 0:1],
                                         None, ALU.is_ge),
         reads=[("tmp8", 0, 8), ("m2r", 0, 1)], writes=[("msk2", 0, 8)])
    S.op("ve", lambda v: v.tensor_mul(exrow[:, :], msk2[:, :], msk1[:, :]),
         reads=[("msk2", 0, 8), ("msk1", 0, 8)], writes=[("exrow", 0, 8)])
    S.op("ve", lambda v: v.tensor_sub(msk2[:, :], msk2[:, :], exrow[:, :]),
         reads=[("msk2", 0, 8), ("exrow", 0, 8)], writes=[("msk2", 0, 8)])
    S.op("ve", lambda v: v.tensor_mul(pkrow[:, 16:24], msk1[:, :], probs[:, :]),
         reads=[("msk1", 0, 8), ("probs", 0, 8)], writes=[("swbuf", 16, 24)])
    S.op("ve", lambda v: v.tensor_mul(pkrow[:, 24:32], msk2[:, :], probs[:, :]),
         reads=[("msk2", 0, 8), ("probs", 0, 8)], writes=[("swbuf", 24, 32)])
    S.op("ve", lambda v: (
        v.tensor_copy(pkrow[:, 0:8], msk1[:, :]),
        v.tensor_copy(pkrow[:, 8:16], msk2[:, :]),
        v.memset(pkrow[:, 32:64], 0.0),
    )[-1], reads=[("msk1", 0, 8), ("msk2", 0, 8)],
        writes=[("swbuf", 0, 16), ("swbuf", 32, 64)])
    for s4 in range(4):
        S.op("ve", lambda v, s4=s4: v.tensor_scalar(
            sendm, pkrow, ohss[0:1, s4:s4 + 1], None, ALU.mult),
            reads=[fsw(PKR), ("ohss", s4, s4 + 1)], writes=[fsw(SDM)])
        S.op("gp", lambda g, s4=s4: g.dma_start(
            out=cc_mk_i[s4:s4 + 1, :], in_=sendm), inc=16, sem="dmki",
            reads=[fsw(SDM)], writes=[("cc_mk_i", s4, s4 + 1)])
    t_cc_mk = S.op("gp", lambda g: g.collective_compute(
        "AllReduce", ALU.add, replica_groups=[list(range(8))],
        ins=[cc_mk_i], outs=[cc_mk_o]), coll="coll_mk",
        reads=[("cc_mk_i", 0, 4)], writes=[("cc_mk_o", 0, 1)])
    t_cc_b2 = S.op("gp", lambda g: g.collective_compute(
        "AllReduce", ALU.add, replica_groups=[list(range(8))],
        ins=[cc_mk_o], outs=[cc_b2_o]), coll="coll_b2",
        reads=[("cc_mk_o", 0, 1)], writes=[("cc_b2_o", 0, 1)])
    _prev_rmv = [None]
    for s4 in range(4):
        _prev_rmv[0] = S.op("gp", lambda g, s4=s4: g.dma_start(
            out=rmv[:, s4 * 64:(s4 + 1) * 64], in_=cc_mk_o[s4:s4 + 1, :]), inc=16,
            sem="drmv", waits=[t_cc_b2, _prev_rmv[0]],
            reads=[("cc_mk_o", 0, 1)],
            writes=[("swbuf", 128 + s4 * 64, 128 + (s4 + 1) * 64)])
    S.op("ve", lambda v: v.memset(selrow, 0.0), writes=[fsw(SELR)])
    for s4 in range(4):
        S.op("ve", lambda v, s4=s4: v.scalar_tensor_tensor(
            selrow, rmv[:, s4 * 64:(s4 + 1) * 64], ohas[0:1, s4:s4 + 1],
            selrow, ALU.mult, ALU.add),
            reads=[("swbuf", 128 + s4 * 64, 128 + (s4 + 1) * 64),
                   ("ohas", s4, s4 + 1), fsw(SELR)],
            writes=[fsw(SELR)])
    S.op("te", lambda t: t.matmul(pMM[1][:, 0:64], onesK[0:1, :], selrow[0:1, :],
                                  start=True, stop=True),
         reads=[("onesK", 0, 128), fsw(SELR)], writes=[("pMM1", 0, 512)])
    S.op("ve", lambda v: v.tensor_copy(rb[:, :], pMM[1][:, 0:64]),
         reads=[("pMM1", 0, 512)], writes=[("rb", 0, 64)])

    def accum_sel(dst, dkey, src_ap_fn, skey_fn, coff):
        S.op("ve", lambda v: v.tensor_scalar(
            dst[:, :], src_ap_fn(0), rb[:, coff:coff + 1], None, ALU.mult),
            reads=[skey_fn(0), ("rb", coff, coff + 1)], writes=[dkey])
        for e in range(1, 8):
            S.op("ve", lambda v, e=e: v.scalar_tensor_tensor(
                dst[:, :], src_ap_fn(e), rb[:, coff + e:coff + e + 1], dst[:, :],
                ALU.mult, ALU.add),
                reads=[skey_fn(e), ("rb", coff + e, coff + e + 1), dkey],
                writes=[dkey])

    accum_sel(w1s1, ("w1s1", 0, 128),
              lambda e: w1b[:, e * 128:(e + 1) * 128],
              lambda e: ("w1b", e * 128, (e + 1) * 128), 0)
    accum_sel(w1s2, ("w1s2", 0, 128),
              lambda e: w1b[:, e * 128:(e + 1) * 128],
              lambda e: ("w1b", e * 128, (e + 1) * 128), 8)
    accum_sel(w2s1, ("w2s1", 0, 128),
              lambda e: w2b[:, e * 128:(e + 1) * 128],
              lambda e: ("w2b", e * 128, (e + 1) * 128), 16)
    accum_sel(w2s2, ("w2s2", 0, 128),
              lambda e: w2b[:, e * 128:(e + 1) * 128],
              lambda e: ("w2b", e * 128, (e + 1) * 128), 24)
    accum_sel(b1s1, ("b1s1", 0, 1), lambda e: b1sb[:, e:e + 1],
              lambda e: ("b1sb", e, e + 1), 0)
    accum_sel(b1s2, ("b1s2", 0, 1), lambda e: b1sb[:, e:e + 1],
              lambda e: ("b1sb", e, e + 1), 8)
    accum_sel(b2c, ("b2c", 0, 1), lambda e: b2sb[:, e:e + 1],
              lambda e: ("b2sb", e, e + 1), 16)
    for e in range(8):
        S.op("ve", lambda v, e=e: v.scalar_tensor_tensor(
            b2c[:, :], b2sb[:, e:e + 1], rb[:, 24 + e:25 + e], b2c[:, :],
            ALU.mult, ALU.add),
            reads=[("b2sb", e, e + 1), ("rb", 24 + e, 25 + e), ("b2c", 0, 1)],
            writes=[("b2c", 0, 1)])

    S.serial = False
    for isb in range(NSB):
        S.op("sy", lambda s, isb=isb: s.dma_start(
            out=ysb.rearrange("p (c t) -> p c t", t=SBW),
            in_=y_sp.rearrange("(c p) t -> p c t", p=P)[:, :, isb * SBW:(isb + 1) * SBW]),
            inc=16, sem="dysb", reads=[("y_sp", isb * SBW, (isb + 1) * SBW)],
            writes=[("ysb", 0, 8192)])
        HB = [pO[0], pO[1], pQK[1]]
        HK = ["pO0", "pO1", "pQK1"]
        for dc in range(NCK):
            b1x = (2 * dc) % 3
            b2x = (2 * dc + 1) % 3
            S.op("te", lambda t, dc=dc, isb=isb, b1x=b1x: t.matmul(
                HB[b1x][:, :], w1s1[:, :],
                xgy[:, dc * T + isb * SBW: dc * T + (isb + 1) * SBW],
                start=True, stop=True),
                reads=[("w1s1", 0, 128), XGY_ALL], writes=[(HK[b1x], 0, 512)])
            S.op("te", lambda t, dc=dc, isb=isb, b2x=b2x: t.matmul(
                HB[b2x][:, :], w1s2[:, :],
                xgy[:, dc * T + isb * SBW: dc * T + (isb + 1) * SBW],
                start=True, stop=True),
                reads=[("w1s2", 0, 128), XGY_ALL], writes=[(HK[b2x], 0, 512)])
            S.op("sc", lambda s, b1x=b1x: s.activation(h1b, HB[b1x][:, :], AF.Gelu,
                                                       bias=b1s1[:, :]),
                 reads=[(HK[b1x], 0, 512), ("b1s1", 0, 1)], writes=[("go1", 0, 512)])
            S.op("sc", lambda s, b2x=b2x: s.activation(h2b, HB[b2x][:, :], AF.Gelu,
                                                       bias=b1s2[:, :]),
                 reads=[(HK[b2x], 0, 512), ("b1s2", 0, 1)], writes=[("go1", 512, 1024)])
            S.op("te", lambda t: t.matmul(pQK[0][:, :], w2s1[:, :], h1b,
                                          start=True, stop=False),
                 reads=[("w2s1", 0, 128), ("go1", 0, 512)], writes=[("pQK0", 0, 512)])
            S.op("te", lambda t: t.matmul(pQK[0][:, :], w2s2[:, :], h2b,
                                          start=False, stop=True),
                 reads=[("w2s2", 0, 128), ("go1", 512, 1024)],
                 writes=[("pQK0", 0, 512)])
            S.op("ve", lambda v, dc=dc: v.scalar_tensor_tensor(
                a1(B_outsb, dc * SBW, SBW), pQK[0][:, :], b2c[:, :],
                ysb[:, dc * SBW:(dc + 1) * SBW], ALU.add, ALU.add),
                reads=[("pQK0", 0, 512), ("b2c", 0, 1),
                       ("ysb", dc * SBW, (dc + 1) * SBW)],
                writes=[fa1(B_outsb, dc * SBW, SBW)])
        for gc in range(NCK):
            slw = gc % 3
            par = gc % 2
            S.op("gp", lambda g, gc=gc, slw=slw: g.dma_start(
                out=ws[:, slw * 2048:(slw + 1) * 2048],
                in_=w3t[gc * P:(gc + 1) * P, :]), inc=16, sem=f"dws{slw}",
                writes=[("ws", slw * 2048, (slw + 1) * 2048)])

            def w3mm(t, slw=slw, par=par):
                ins = None
                for dc in range(NCK):
                    ins = t.matmul(
                        pMM[par][:, :],
                        ws[:, slw * 2048 + dc * 128:slw * 2048 + (dc + 1) * 128],
                        a1(B_outsb, dc * SBW, SBW),
                        start=(dc == 0), stop=(dc == NCK - 1))
                return ins
            S.op("te", w3mm,
                 reads=[("ws", slw * 2048, (slw + 1) * 2048), fa1(B_outsb)],
                 writes=[(f"pMM{par}", 0, 512)])
            S.op("ve", lambda v, gc=gc, par=par: v.scalar_tensor_tensor(
                ysb[:, gc * SBW:(gc + 1) * SBW], pMM[par][:, :], b3sb[:, gc:gc + 1],
                ysb[:, gc * SBW:(gc + 1) * SBW], ALU.add, ALU.add),
                reads=[(f"pMM{par}", 0, 512), ("b3sb", gc, gc + 1),
                       ("ysb", gc * SBW, (gc + 1) * SBW)],
                writes=[("ysb", gc * SBW, (gc + 1) * SBW)])
        def d_store(gc, isb=isb):
            pass
        ln_stats_norm(d_store)
        S.op("sy", lambda s, isb=isb: s.dma_start(
            out=yTo.rearrange("(c p) t -> p c t", p=P)[:, :, isb * SBW:(isb + 1) * SBW],
            in_=ysb.rearrange("p (c t) -> p c t", t=SBW)), inc=16, sem="dyo",
            reads=[("ysb", 0, 8192)])

    final_waits = [("dyo", S.cnt.get("dyo", 0))]

    blk = es.enter_context(nc.Block())

    def emit(eng_name):
        def body(e):
            for fn, ws_, inc, coll, semname in S.ops[eng_name]:
                for (weng, val) in ws_:
                    e.wait_ge(sems[weng], val)
                ins = fn(e)
                ins.then_inc(sems[semname], 1 if coll is not None else inc)
            if eng_name == "sy":
                for sname, val in final_waits:
                    e.wait_ge(sems[sname], val)
        return body

    blk.sync(emit("sy"))
    blk.gpsimd(emit("gp"))
    blk.tensor(emit("te"))
    blk.scalar(emit("sc"))
    blk.vector(emit("ve"))

    es.close()
    return nc


_NC_CACHE = None


def _get_nc():
    global _NC_CACHE
    if _NC_CACHE is None:
        _NC_CACHE = build_nc()
    return _NC_CACHE


def kernel(L0_attn_hidden_states, L0_attn_slope_rate, L0_attn_w_qkv, L0_attn_w_gate,
           L0_attn_w_out, L0_attn_norm_w, L0_ffn_switch_w, L0_ffn_switch_b,
           L0_ffn_w1, L0_ffn_b1, L0_ffn_w2, L0_ffn_b2, L0_ffn_w3, L0_ffn_b3):
    x = np.ascontiguousarray(np.asarray(L0_attn_hidden_states, np.float32))
    slopes = np.asarray(L0_attn_slope_rate, np.float32).reshape(1, 16)
    wqkv = np.asarray(L0_attn_w_qkv, np.float32)
    wgate = np.asarray(L0_attn_w_gate, np.float32)
    wout = np.asarray(L0_attn_w_out, np.float32)
    nw = np.asarray(L0_attn_norm_w, np.float32)
    sw = np.asarray(L0_ffn_switch_w, np.float32)
    swb = np.asarray(L0_ffn_switch_b, np.float32).reshape(1, 8)
    w1 = np.asarray(L0_ffn_w1, np.float32)
    b1 = np.asarray(L0_ffn_b1, np.float32)
    w2 = np.asarray(L0_ffn_w2, np.float32)
    b2 = np.asarray(L0_ffn_b2, np.float32)
    w3 = np.asarray(L0_ffn_w3, np.float32)
    b3 = np.asarray(L0_ffn_b3, np.float32)

    def slab(wT, noc):
        t = wT.reshape(16, 128, noc, 128).transpose(2, 1, 0, 3)
        return np.ascontiguousarray(t.reshape(noc * 128, 16 * 128))

    qkvt = slab(wqkv.T, 48)
    gatet = slab(wgate.T, 16)
    woutt = np.ascontiguousarray(
        wout.T.reshape(16, 128, 16, 128).transpose(2, 1, 0, 3).reshape(2048, 2048))
    w3t = np.ascontiguousarray(
        w3.T.reshape(16, 128, 16, 128).transpose(2, 1, 0, 3).reshape(2048, 2048))
    swf = sw.reshape(8, 2048, 16, 128).transpose(0, 2, 3, 1)
    swf = swf.reshape(8, 2, 8, 128, 4, 512).transpose(0, 4, 1, 3, 2, 5)
    swt = np.ascontiguousarray(swf.reshape(8 * 4 * 2 * 128, 8 * 512))
    w1t = np.ascontiguousarray(w1.transpose(0, 2, 1).reshape(8 * 128, 128))
    w2t = np.ascontiguousarray(w2.transpose(0, 2, 1).reshape(8 * 128, 128))
    b1t = np.ascontiguousarray(b1.T)
    b2t = np.ascontiguousarray(b2.T)
    b3fm = np.ascontiguousarray(b3.reshape(16, 128).T)
    nwfm = np.ascontiguousarray(nw.reshape(16, 128).T)

    shared = dict(qkvt=qkvt, gatet=gatet, woutt=woutt, w3t=w3t, swt=swt,
                  w1t=w1t, w2t=w2t, b1t=b1t, b2t=b2t, b3fm=b3fm, nwfm=nwfm,
                  swbr=swb, slr=slopes)
    in_maps = []
    for c in range(8):
        b_, half = c // 2, c % 2
        xc = x[b_, half * 2048:(half + 1) * 2048, :]
        xTc = np.ascontiguousarray(xc.T)
        ohs_ = np.zeros((1, 4), np.float32)
        ohr_ = np.zeros((1, 4), np.float32)
        oha_ = np.zeros((1, 4), np.float32)
        oha_[0, b_] = 1.0
        if half == 0:
            ohs_[0, b_] = 1.0
        else:
            ohr_[0, b_] = 1.0
        m = dict(shared)
        m.update(xT=xTc, ohs=ohs_, ohr=ohr_, oha=oha_)
        in_maps.append(m)

    nc = _get_nc()
    res = run_bass_kernel_spmd(nc, in_maps, core_ids=list(range(8)))
    out = np.empty((4, 4096, 2048), np.float32)
    for c in range(8):
        b_, half = c // 2, c % 2
        out[b_, half * 2048:(half + 1) * 2048, :] = res.results[c]["yTo"].T
    return out


# version 23
# speedup vs baseline: 1.0153x; 1.0153x over previous
import contextlib
import numpy as np
import concourse.bass as bass
import concourse.mybir as mybir
from concourse.bass_utils import run_bass_kernel_spmd

F32 = mybir.dt.float32
BF16 = mybir.dt.bfloat16
AF = mybir.ActivationFunctionType
ALU = mybir.AluOpType
AX = mybir.AxisListType

P = 128
T = 2048
NH = 16
NB = 16
NCK = 16
NSB = 4
SBW = 512
EPS_RMS = 1e-6
EPS_LN = 1e-5
ENGS = ("sy", "gp", "te", "sc", "ve")


SERIALIZE = False


class Sched:

    def __init__(self):
        self.serial = False
        self.ops = {e: [] for e in ENGS}
        self.cnt = {}
        self.colls_seen = []
        self.wr = {}
        self.rd = {}

    def _merge(self, ws, tok):
        if tok is None:
            return
        e2, v2 = tok
        lim = 1 if e2.startswith("coll") else self.cnt.get(e2, 0)
        assert v2 <= lim, f"future token {tok}"
        ws[e2] = max(ws.get(e2, 0), v2)

    def op(self, eng, fn, waits=(), inc=1, coll=None, reads=(), writes=(),
           sem=None):
        ws = {}
        for w in waits:
            self._merge(ws, w)
        for (t, lo, hi) in reads:
            for s in self.wr.get(t, ()):
                if s[0] < hi and lo < s[1]:
                    self._merge(ws, s[2])
        for (t, lo, hi) in writes:
            for s in self.wr.get(t, ()):
                if s[0] < hi and lo < s[1]:
                    self._merge(ws, s[2])
            new_rd = []
            for r in self.rd.get(t, ()):
                if r[0] < hi and lo < r[1]:
                    for it in r[2].items():
                        self._merge(ws, it)
                    if r[0] < lo:
                        new_rd.append([r[0], lo, dict(r[2])])
                    if hi < r[1]:
                        new_rd.append([hi, r[1], dict(r[2])])
                else:
                    new_rd.append(r)
            self.rd[t] = new_rd
        if SERIALIZE or self.serial:
            for e2, v2 in self.cnt.items():
                if v2 > 0:
                    ws[e2] = v2
            for cname in self.colls_seen:
                ws[cname] = 1
        sem = sem or eng
        if coll is not None:
            tok = (coll, 1)
            self.colls_seen.append(coll)
        else:
            self.cnt[sem] = self.cnt.get(sem, 0) + inc
            tok = (sem, self.cnt[sem])
        for (t, lo, hi) in reads:
            rl = self.rd.setdefault(t, [])
            if rl and rl[-1][0] == lo and rl[-1][1] == hi:
                rl[-1][2][tok[0]] = max(rl[-1][2].get(tok[0], 0), tok[1])
            else:
                rl.append([lo, hi, {tok[0]: tok[1]}])
        for (t, lo, hi) in writes:
            segs = self.wr.setdefault(t, [])
            out = []
            for s in segs:
                if s[0] < hi and lo < s[1]:
                    if s[0] < lo:
                        out.append([s[0], lo, s[2]])
                    if hi < s[1]:
                        out.append([hi, s[1], s[2]])
                else:
                    out.append(s)
            out.append([lo, hi, tok])
            self.wr[t] = out
        self.ops[eng].append((fn, sorted(ws.items()), inc, coll, tok[0]))
        return tok


def build_nc():
    nc = bass.Bass("TRN2", target_bir_lowering=False)
    dt = nc.dram_tensor
    xT = dt("xT", [2048, T], F32, kind="ExternalInput").ap()
    qkvt = dt("qkvt", [48 * P, 2048], F32, kind="ExternalInput").ap()
    gatet = dt("gatet", [16 * P, 2048], F32, kind="ExternalInput").ap()
    woutt = dt("woutt", [16 * P, 2048], F32, kind="ExternalInput").ap()
    w3t = dt("w3t", [16 * P, 2048], F32, kind="ExternalInput").ap()
    swt = dt("swt", [8 * 4 * 2 * P, 4096], F32, kind="ExternalInput").ap()
    w1t = dt("w1t", [8 * P, 128], F32, kind="ExternalInput").ap()
    w2t = dt("w2t", [8 * P, 128], F32, kind="ExternalInput").ap()
    b1t = dt("b1t", [P, 8], F32, kind="ExternalInput").ap()
    b2t = dt("b2t", [P, 8], F32, kind="ExternalInput").ap()
    b3fm = dt("b3fm", [P, 16], F32, kind="ExternalInput").ap()
    nwfm = dt("nwfm", [P, 16], F32, kind="ExternalInput").ap()
    swbr = dt("swbr", [1, 8], F32, kind="ExternalInput").ap()
    slr = dt("slr", [1, 16], F32, kind="ExternalInput").ap()
    ohs = dt("ohs", [1, 4], F32, kind="ExternalInput").ap()
    ohr = dt("ohr", [1, 4], F32, kind="ExternalInput").ap()
    oha = dt("oha", [1, 4], F32, kind="ExternalInput").ap()
    yTo = dt("yTo", [2048, T], F32, kind="ExternalOutput").ap()
    out_sp = dt("out_sp", [2048, T], BF16, kind="Internal").ap()
    qx_sp = dt("qx_sp", [2048, T], BF16, kind="Internal").ap()
    y_sp = dt("y_sp", [2048, T], F32, kind="Internal").ap()
    cc_kv_i = dt("cc_kv_i", [4 * 2048, 128], BF16, kind="Internal").ap()
    cc_kv_o = dt("cc_kv_o", [4 * 2048, 128], BF16, kind="Internal").ap()
    cc_mk_i = dt("cc_mk_i", [4, 64], F32, kind="Internal").ap()
    cc_mk_o = dt("cc_mk_o", [4, 64], F32, kind="Internal").ap()
    cc_b1_o = dt("cc_b1_o", [4, 128], BF16, kind="Internal").ap()
    gate_bf = dt("gate_bf", [16 * P, 2048], BF16, kind="Internal").ap()
    wout_bf = dt("wout_bf", [16 * P, 2048], BF16, kind="Internal").ap()
    w3_bf = dt("w3_bf", [16 * P, 2048], BF16, kind="Internal").ap()
    cc_b2_o = dt("cc_b2_o", [4, 64], F32, kind="Internal").ap()

    es = contextlib.ExitStack()

    def sb(name, shape, dtype):
        return es.enter_context(nc.sbuf_tensor(name, shape, dtype))

    xgy = sb("xgy", [P, NCK * T], BF16)
    ar1 = sb("ar1", [P, 16384], BF16)
    ar2 = sb("ar2", [P, 16384], BF16)
    ysb = sb("ysb", [P, 8192], F32)
    ws = sb("ws", [P, 3 * 2048], BF16)
    swbuf = sb("swbuf", [P, 2 * 1024], F32)
    idxT = sb("idxT", [P, 128], F32)
    idxQ = sb("idxQ", [P, 128], F32)
    iotaJ = sb("iotaJ", [P, 16], F32)
    kIdx = sb("kIdx", [P, 1], F32)
    ident = sb("ident", [P, 128], BF16)
    decT = sb("decT", [P, 2 * 128], BF16)
    qdecT = sb("qdecT", [P, 2 * 128], BF16)
    prodf = sb("prodf", [P, 2 * 512], F32)
    accsc = sb("accsc", [1, 32], F32)
    kdecc = sb("kdecc", [P, 2], F32)
    dpow = sb("dpow", [P, 2 * 16], F32)
    slb = sb("slb", [P, 16], F32)
    nslb = sb("nslb", [P, 16], F32)
    s128 = sb("s128", [P, 16], F32)
    bdecb = sb("bdecb", [P, 16], F32)
    ohb = sb("ohb", [P, 12], F32)
    kvf = sb("kvf", [P, 128], F32)
    kvb = sb("kvb", [P, 128], BF16)
    kvsel = sb("kvsel", [P, 2048], BF16)
    onesb = sb("onesb", [P, 1], BF16)
    onesK = sb("onesK", [1, P], F32)
    onesM = sb("onesM", [P, 1], F32)
    nwsb = sb("nwsb", [P, 16], F32)
    b3sb = sb("b3sb", [P, 16], F32)
    b1sb = sb("b1sb", [P, 8], F32)
    b2sb = sb("b2sb", [P, 8], F32)
    w1b = sb("w1b", [P, 8 * 128], BF16)
    w2b = sb("w2b", [P, 8 * 128], BF16)
    w1s1 = sb("w1s1", [P, 128], BF16)
    w1s2 = sb("w1s2", [P, 128], BF16)
    w2s1 = sb("w2s1", [P, 128], BF16)
    w2s2 = sb("w2s2", [P, 128], BF16)
    b1s1 = sb("b1s1", [P, 1], F32)
    b1s2 = sb("b1s2", [P, 1], F32)
    b2c = sb("b2c", [P, 1], F32)
    qkTs = sb("qkTs", [P, 2 * 128], BF16)
    qps = sb("qps", [P, 2 * 128], BF16)
    outsq = sb("outsq", [P, 2 * 512], BF16)
    go1 = sb("go1", [P, 2 * 512], BF16)
    lnb = sb("lnb", [P, 2 * 512], BF16)
    sqb = sb("sqb", [P, 2 * 512], BF16)
    h1b = go1[:, 0:512]
    h2b = go1[:, 512:1024]
    accR = sb("accR", [P, 64], F32)
    accE = sb("accE", [P, 8], F32)
    rb = sb("rb", [P, 64], F32)
    slrs = sb("slrs", [1, 16], F32)
    swbs = sb("swbs", [1, 8], F32)
    ohss = sb("ohss", [1, 4], F32)
    ohrs = sb("ohrs", [1, 4], F32)
    ohas = sb("ohas", [1, 4], F32)
    ohrow = sb("ohrow", [1, 12], F32)
    rowsA = swbuf[0:1, 0:512]
    rowsB = swbuf[0:1, 512:1024]
    rowsC = swbuf[0:1, 1024:1536]
    rowsD = swbuf[0:1, 1536:2048]
    lgrow = sb("lgrow", [1, 8], F32)
    exrow = sb("exrow", [1, 8], F32)
    probs = sb("probs", [1, 8], F32)
    msk1 = sb("msk1", [1, 8], F32)
    msk2 = sb("msk2", [1, 8], F32)
    tmp8 = sb("tmp8", [1, 8], F32)
    m1r = sb("m1r", [1, 1], F32)
    m2r = sb("m2r", [1, 1], F32)
    srow = sb("srow", [1, 1], F32)
    pkrow = swbuf[0:1, 0:64]
    sendm = swbuf[0:1, 64:128]
    rmv = swbuf[0:1, 128:384]
    selrow = swbuf[0:1, 384:448]

    pMM = [es.enter_context(nc.psum_tensor(f"pMM{i}", [P, 512], F32)) for i in range(2)]
    pQK = [es.enter_context(nc.psum_tensor(f"pQK{i}", [P, 512], F32)) for i in range(2)]
    pO = [es.enter_context(nc.psum_tensor(f"pO{i}", [P, 512], F32)) for i in range(2)]
    pT = [es.enter_context(nc.psum_tensor(f"pT{i}", [P, 1024], BF16)) for i in range(2)]

    _sem_names = (list(ENGS) +
                  ["coll_kv", "coll_mk", "coll_b1", "coll_b2",
                   "dx", "dsm", "dw12", "dws0", "dws1", "dws2", "dsw0", "dsw1",
                   "doutw", "dqxw", "doutsb", "dqxsb", "dysb", "dywr",
                   "dcci", "drecv", "dmki", "drmv", "dyo"])
    sems = {n: es.enter_context(nc.semaphore(f"sem_{n}")) for n in _sem_names}

    S = Sched()

    A_qf = (0, 2048)
    A_kf = (2048, 4096)
    A_ktm = (4096, 6144)
    A_vt = (6144, 8192)
    A_qx = (8192, 10240)
    A_outb = (10240, 12288)
    A_kvs = (12288, 14336)
    B_outsb = (0, 8192)
    B_gateb = (0, 8192)
    B_go2b = (8192, 16384)
    C_junk = (0, 8192)

    def a1(rng, j0=0, w=None):
        a, b = rng
        return ar1[:, a:b] if w is None else ar1[:, a + j0:a + j0 + w]

    def a2(rng, j0=0, w=None):
        a, b = rng
        return ar2[:, a:b] if w is None else ar2[:, a + j0:a + j0 + w]

    def fa1(rng, j0=0, w=None):
        a, b = rng
        return ("ar1", a + j0, b if w is None else a + j0 + w)

    def fa2(rng, j0=0, w=None):
        a, b = rng
        return ("ar2", a + j0, b if w is None else a + j0 + w)

    SW0, SW1 = (0, 1024), (1024, 2048)
    R_A, R_B, R_C, R_D = (0, 512), (512, 1024), (1024, 1536), (1536, 2048)
    PKR, SDM, RMR, SELR = (0, 64), (64, 128), (128, 384), (384, 448)

    def fsw(rng):
        return ("swbuf", rng[0], rng[1])

    XGY_ALL = ("xgy", 0, NCK * T)

    def _preload(s):
        ins = None
        for f in (AF.Silu, AF.Sigmoid, AF.Exp, AF.Gelu, AF.Sqrt, AF.Square,
                  AF.Identity):
            ins = s.activation(swbuf[0:1, 1536:1544], swbuf[0:1, 1544:1552], f)
        return ins
    t_pre = S.op("sc", _preload, writes=[("acttab", 0, 1), ("swbuf", 1536, 1544)],
                 reads=[("swbuf", 1544, 1552)])

    t_gp_iota = S.op("gp", lambda g: (
        g.iota(idxT[:, :], [[1, 128]], base=0, channel_multiplier=-1,
               allow_small_or_imprecise_dtypes=True),
        g.affine_select(idxT[:, :], idxT[:, :], [[1, 128]], ALU.is_ge, 1e30,
                        base=0, channel_multiplier=-1),
        g.iota(idxQ[:, :], [[1, 128]], base=1, channel_multiplier=0,
               allow_small_or_imprecise_dtypes=True),
        g.iota(iotaJ[:, :], [[1, 16]], base=0, channel_multiplier=0,
               allow_small_or_imprecise_dtypes=True),
        g.iota(kIdx[:, :], [[1, 1]], base=127, channel_multiplier=-1,
               allow_small_or_imprecise_dtypes=True),
        g.memset(ident[:, :], 1.0),
        g.affine_select(ident[:, :], ident[:, :], [[1, 128]], ALU.is_equal, 0.0,
                        base=0, channel_multiplier=-1),
        g.memset(onesb[:, :], 1.0),
        g.memset(onesK[:, :], 1.0),
        g.memset(onesM[:, :], 1.0),
    )[-1], writes=[("idxT", 0, 128), ("idxQ", 0, 128), ("iotaJ", 0, 16),
                   ("kIdx", 0, 1), ("ident", 0, 128), ("onesb", 0, 1),
                   ("onesK", 0, 128), ("onesM", 0, 1)])

    for i in range(4):
        S.op("gp", lambda g, i=i: g.dma_start(
            out=xgy.rearrange("p (c t) -> p c t", t=T)[:, 4 * i:4 * i + 4, :],
            in_=xT.rearrange("(c p) t -> p c t", p=P)[:, 4 * i:4 * i + 4, :]),
            inc=16, sem="dx", waits=[t_pre],
            writes=[("xgy", i * 4 * T, (i + 1) * 4 * T)])

    _small_loads = [
        (slrs, slr, "slrs", 16), (nwsb, nwfm, "nwsb", 16), (b3sb, b3fm, "b3sb", 16),
        (swbs, swbr, "swbs", 8), (b1sb, b1t, "b1sb", 8), (b2sb, b2t, "b2sb", 8),
        (ohss, ohs, "ohss", 4), (ohrs, ohr, "ohrs", 4), (ohas, oha, "ohas", 4)]
    for _dst, _src, _k, _n in _small_loads:
        S.op("sy", lambda s, d=_dst, i=_src: s.dma_start(out=d[:, :], in_=i),
             inc=16, sem="dsm", waits=[t_pre], writes=[(_k, 0, _n)])

    S.op("gp", lambda g: g.dma_start(
        out=w1b.rearrange("p (e f) -> p e f", e=8),
        in_=w1t.rearrange("(e d) f -> d e f", d=P)), inc=16, sem="dw12",
        waits=[t_pre], writes=[("w1b", 0, 1024)])
    S.op("gp", lambda g: g.dma_start(
        out=w2b.rearrange("p (e f) -> p e f", e=8),
        in_=w2t.rearrange("(e d) f -> d e f", d=P)), inc=16, sem="dw12",
        writes=[("w2b", 0, 1024)])

    S.op("te", lambda t: t.matmul(pMM[0][:, 0:16], onesK[0:1, :],
                                  slrs[0:1, :], start=True, stop=True),
         reads=[("onesK", 0, 128), ("slrs", 0, 16)], writes=[("pMM0", 0, 512)])
    S.op("ve", lambda v: v.tensor_copy(slb[:, :], pMM[0][:, 0:16]),
         reads=[("pMM0", 0, 512)], writes=[("slb", 0, 16)])
    S.op("ve", lambda v: v.tensor_scalar(nslb[:, :], slb[:, :], -1.0, None,
                                         ALU.mult),
         reads=[("slb", 0, 16)], writes=[("nslb", 0, 16)])
    S.op("ve", lambda v: v.tensor_scalar(s128[:, :], slb[:, :], -128.0, None,
                                         ALU.mult),
         reads=[("slb", 0, 16)], writes=[("s128", 0, 16)])
    S.op("sc", lambda s: s.activation(bdecb[:, :], nslb[:, :], AF.Exp, scale=128.0),
         reads=[("nslb", 0, 16)], writes=[("bdecb", 0, 16)])
    S.op("ve", lambda v: (
        v.tensor_copy(ohrow[:, 0:4], ohss[:, :]),
        v.tensor_copy(ohrow[:, 4:8], ohrs[:, :]),
        v.tensor_copy(ohrow[:, 8:12], ohas[:, :]),
    )[-1], reads=[("ohss", 0, 4), ("ohrs", 0, 4), ("ohas", 0, 4)],
        writes=[("ohrow", 0, 12)])
    S.op("te", lambda t: t.matmul(pMM[1][:, 0:12], onesK[0:1, :],
                                  ohrow[0:1, :], start=True, stop=True),
         reads=[("onesK", 0, 128), ("ohrow", 0, 12)], writes=[("pMM1", 0, 512)])
    S.op("ve", lambda v: v.tensor_copy(ohb[:, :], pMM[1][:, 0:12]),
         reads=[("pMM1", 0, 512)], writes=[("ohb", 0, 12)])

    SER_A1 = SER_A2 = SER_A3 = SER_A4 = False
    for h in range(NH):
        sl = h % 2
        sq, sk, sv = (0, 1, 2) if h % 2 == 0 else (2, 0, 1)
        S.serial = SER_A4
        S.op("gp", lambda g, h=h, sq=sq: g.dma_start(
            out=ws[:, sq * 2048:(sq + 1) * 2048],
            in_=qkvt[(3 * h) * P:(3 * h + 1) * P, :]), inc=16, sem=f"dws{sq}",
            waits=[t_pre if h == 0 else None],
            writes=[("ws", sq * 2048, (sq + 1) * 2048)])
        S.op("gp", lambda g, h=h, sk=sk: g.dma_start(
            out=ws[:, sk * 2048:(sk + 1) * 2048],
            in_=qkvt[(3 * h + 1) * P:(3 * h + 2) * P, :]), inc=16, sem=f"dws{sk}",
            writes=[("ws", sk * 2048, (sk + 1) * 2048)])
        S.op("gp", lambda g, h=h, sv=sv: g.dma_start(
            out=ws[:, sv * 2048:(sv + 1) * 2048],
            in_=qkvt[(3 * h + 2) * P:(3 * h + 3) * P, :]), inc=16, sem=f"dws{sv}",
            writes=[("ws", sv * 2048, (sv + 1) * 2048)])

        S.op("sc", lambda s, h=h, sl=sl: (
            s.activation(decT[:, sl * 128:(sl + 1) * 128], idxT[:, :], AF.Exp,
                         scale=nslb[:, h:h + 1]),
            s.activation(qdecT[:, sl * 128:(sl + 1) * 128], idxQ[:, :], AF.Exp,
                         scale=nslb[:, h:h + 1]),
            s.activation(kdecc[:, sl:sl + 1], kIdx[:, :], AF.Exp,
                         scale=nslb[:, h:h + 1]),
            s.activation(dpow[:, sl * 16:(sl + 1) * 16], iotaJ[:, :], AF.Exp,
                         scale=s128[:, h:h + 1]),
        )[-1],
            reads=[("idxT", 0, 128), ("idxQ", 0, 128), ("kIdx", 0, 1),
                   ("iotaJ", 0, 16), ("nslb", 0, 16), ("s128", 0, 16)],
            writes=[("decT", sl * 128, (sl + 1) * 128),
                    ("qdecT", sl * 128, (sl + 1) * 128),
                    ("kdecc", sl, sl + 1), ("dpow", sl * 16, (sl + 1) * 16)])
        S.serial = SER_A1
        for (slot, dstrng) in ((sq, A_qf), (sk, A_kf)):
            for isb in range(NSB):
                par = isb % 2

                def mmgrp(t, slot=slot, isb=isb, par=par):
                    ins = None
                    for fc in range(NCK):
                        ins = t.matmul(
                            pMM[par][:, :],
                            ws[:, slot * 2048 + fc * 128:slot * 2048 + (fc + 1) * 128],
                            xgy[:, fc * T + isb * SBW: fc * T + (isb + 1) * SBW],
                            start=(fc == 0), stop=(fc == NCK - 1))
                    return ins
                S.op("te", mmgrp,
                     reads=[("ws", slot * 2048, (slot + 1) * 2048), XGY_ALL],
                     writes=[(f"pMM{par}", 0, 512)])
                S.op("sc", lambda s, isb=isb, par=par, dstrng=dstrng: s.activation(
                    a1(dstrng, isb * SBW, SBW), pMM[par][:, :], AF.Silu),
                    reads=[(f"pMM{par}", 0, 512)],
                    writes=[fa1(dstrng, isb * SBW, SBW)])

        S.serial = SER_A1
        for tc in range(NB):
            par = tc % 2

            def mmv(t, tc=tc, par=par, slot=sv):
                ins = None
                for fc in range(NCK):
                    ins = t.matmul(
                        pMM[par][:, 0:128],
                        xgy[:, fc * T + tc * 128: fc * T + (tc + 1) * 128],
                        ws[:, slot * 2048 + fc * 128:slot * 2048 + (fc + 1) * 128],
                        start=(fc == 0), stop=(fc == NCK - 1))
                return ins
            S.op("te", mmv,
                 reads=[("ws", sv * 2048, (sv + 1) * 2048), XGY_ALL],
                 writes=[(f"pMM{par}", 0, 512)])
            S.op("sc", lambda s, tc=tc, par=par: s.activation(
                a1(A_vt, tc * 128, 128), pMM[par][:, 0:128], AF.Silu),
                reads=[(f"pMM{par}", 0, 512)], writes=[fa1(A_vt, tc * 128, 128)])

        S.serial = SER_A2
        for j in range(NB):
            par = j % 2
            S.op("te", lambda t, j=j, par=par: t.transpose(
                pT[par][:, 0:128], a1(A_kf, j * 128, 128), ident[:, :]),
                reads=[fa1(A_kf, j * 128, 128), ("ident", 0, 128)],
                writes=[(f"pT{par}", 0, 1024)])
            S.op("ve", lambda v, j=j, par=par, sl=sl: v.tensor_scalar(
                a1(A_ktm, j * 128, 128), pT[par][:, 0:128], kdecc[:, sl:sl + 1],
                None, ALU.mult),
                reads=[(f"pT{par}", 0, 1024), ("kdecc", sl, sl + 1)],
                writes=[fa1(A_ktm, j * 128, 128)])

        S.serial = SER_A4
        S.op("gp", lambda g: g.memset(kvf[:, :], 0.0), writes=[("kvf", 0, 128)])
        S.op("ve", lambda v: v.memset(kvb[:, :], 0.0), writes=[("kvb", 0, 128)])

        S.serial = SER_A3

        def mm1_op(j):
            par = j % 2
            S.op("te", lambda t, j=j, par=par: t.matmul(
                pQK[par][:, 0:128], a1(A_kf, j * 128, 128), a1(A_qf, j * 128, 128),
                start=True, stop=True),
                reads=[fa1(A_kf, j * 128, 128), fa1(A_qf, j * 128, 128)],
                writes=[(f"pQK{par}", 0, 512)])

        mm1_op(0)
        for j in range(NB):
            par = j % 2
            S.op("ve", lambda v, par=par, sl=sl: v.tensor_tensor(
                qkTs[:, par * 128:(par + 1) * 128], pQK[par][:, 0:128],
                decT[:, sl * 128:(sl + 1) * 128], ALU.mult),
                reads=[(f"pQK{par}", 0, 512), ("decT", sl * 128, (sl + 1) * 128)],
                writes=[("qkTs", par * 128, (par + 1) * 128)])
            S.op("ve", lambda g, j=j, par=par, sl=sl: g.tensor_tensor(
                qps[:, par * 128:(par + 1) * 128], a1(A_qf, j * 128, 128),
                qdecT[:, sl * 128:(sl + 1) * 128], ALU.mult),
                reads=[fa1(A_qf, j * 128, 128), ("qdecT", sl * 128, (sl + 1) * 128)],
                writes=[("qps", par * 128, (par + 1) * 128)])
            S.op("ve", lambda g, j=j, par=par, sl=sl: g.tensor_scalar(
                a1(A_qx, j * 128, 128), qps[:, par * 128:(par + 1) * 128],
                dpow[:, sl * 16 + j:sl * 16 + j + 1], None, ALU.mult),
                reads=[("qps", par * 128, (par + 1) * 128),
                       ("dpow", sl * 16 + j, sl * 16 + j + 1)],
                writes=[fa1(A_qx, j * 128, 128)])
            if j + 1 < NB:
                mm1_op(j + 1)
            S.op("te", lambda t, j=j, par=par: t.matmul(
                pO[par][:, 0:128], a1(A_vt, j * 128, 128),
                qkTs[:, par * 128:(par + 1) * 128], start=True, stop=False),
                reads=[fa1(A_vt, j * 128, 128), ("qkTs", par * 128, (par + 1) * 128)],
                writes=[(f"pO{par}", 0, 512)])
            S.op("te", lambda t, par=par: t.matmul(
                pO[par][:, 0:128], kvb[:, :], qps[:, par * 128:(par + 1) * 128],
                start=False, stop=True),
                reads=[("kvb", 0, 128), ("qps", par * 128, (par + 1) * 128)],
                writes=[(f"pO{par}", 0, 512)])
            S.op("sc", lambda s, j=j, par=par: s.copy(
                a1(A_outb, j * 128, 128), pO[par][:, 0:128]),
                reads=[(f"pO{par}", 0, 512)], writes=[fa1(A_outb, j * 128, 128)])
            S.op("te", lambda t, j=j, par=par: t.matmul(
                pMM[par][:, 0:128], a1(A_ktm, j * 128, 128), a1(A_vt, j * 128, 128),
                start=True, stop=True),
                reads=[fa1(A_ktm, j * 128, 128), fa1(A_vt, j * 128, 128)],
                writes=[(f"pMM{par}", 0, 512)])
            S.op("ve", lambda v, h=h, par=par: v.scalar_tensor_tensor(
                kvf[:, :], kvf[:, :], bdecb[:, h:h + 1], pMM[par][:, 0:128],
                ALU.mult, ALU.add),
                reads=[("kvf", 0, 128), ("bdecb", 0, 16), (f"pMM{par}", 0, 512)],
                writes=[("kvf", 0, 128)])
            S.op("ve", lambda g: g.tensor_copy(kvb[:, :], kvf[:, :]),
                 reads=[("kvf", 0, 128)], writes=[("kvb", 0, 128)])

        S.serial = SER_A4
        S.op("sy", lambda s, h=h: s.dma_start(
            out=out_sp[h * P:(h + 1) * P, :], in_=a1(A_outb)), inc=16, sem="doutw",
            reads=[fa1(A_outb)], writes=[("out_sp", 0, 1)])
        S.op("sc", lambda s, h=h: s.dma_start(
            out=qx_sp[h * P:(h + 1) * P, :], in_=a1(A_qx)), inc=16, sem="dqxw",
            reads=[fa1(A_qx)], writes=[("qx_sp", 0, 1)])
        S.op("ve", lambda v, h=h: v.tensor_copy(
            a1(A_kvs, h * 128, 128), kvf[:, :]),
            reads=[("kvf", 0, 128)], writes=[fa1(A_kvs, h * 128, 128)])

    S.serial = False
    for s4 in range(4):
        S.op("ve", lambda v, s4=s4: v.tensor_scalar(
            a1(A_qx), a1(A_kvs), ohb[:, s4:s4 + 1], None, ALU.mult),
            reads=[fa1(A_kvs), ("ohb", s4, s4 + 1)], writes=[fa1(A_qx)])
        S.op("gp", lambda g, s4=s4: g.dma_start(
            out=cc_kv_i[s4 * 2048:(s4 + 1) * 2048, :].rearrange("(h d) e -> d h e", d=P),
            in_=a1(A_qx).rearrange("p (h e) -> p h e", e=128)), inc=16, sem="dcci",
            reads=[fa1(A_qx)], writes=[("cc_kv_i", s4, s4 + 1)])
    t_cc_kv = S.op("gp", lambda g: g.collective_compute(
        "AllReduce", ALU.add, replica_groups=[list(range(8))],
        ins=[cc_kv_i], outs=[cc_kv_o]), coll="coll_kv",
        reads=[("cc_kv_i", 0, 4)], writes=[("cc_kv_o", 0, 1)])
    t_cc_b1 = S.op("gp", lambda g: g.collective_compute(
        "AllReduce", ALU.add, replica_groups=[list(range(8))],
        ins=[cc_kv_o[0:4, :]], outs=[cc_b1_o]), coll="coll_b1",
        reads=[("cc_kv_o", 0, 1)], writes=[("cc_b1_o", 0, 1)])
    _prev_recv = [None]
    for s4 in range(4):
        _prev_recv[0] = S.op("sc", lambda g, s4=s4: g.dma_start(
            out=ar1[:, 8192 + s4 * 2048:8192 + (s4 + 1) * 2048].rearrange(
                "p (h e) -> p h e", e=128),
            in_=cc_kv_o[s4 * 2048:(s4 + 1) * 2048, :].rearrange("(h d) e -> d h e", d=P)),
            inc=16, sem="drecv", waits=[t_cc_b1, _prev_recv[0]],
            reads=[("cc_kv_o", 0, 1)],
            writes=[("ar1", 8192 + s4 * 2048, 8192 + (s4 + 1) * 2048)])
    S.op("ve", lambda v: v.memset(kvsel[:, :], 0.0), writes=[("kvsel", 0, 2048)])
    for s4 in range(4):
        S.op("ve", lambda v, s4=s4: v.scalar_tensor_tensor(
            kvsel[:, :], ar1[:, 8192 + s4 * 2048:8192 + (s4 + 1) * 2048],
            ohb[:, 4 + s4:5 + s4],
            kvsel[:, :], ALU.mult, ALU.add),
            reads=[("ar1", 8192 + s4 * 2048, 8192 + (s4 + 1) * 2048),
                   ("ohb", 4 + s4, 5 + s4), ("kvsel", 0, 2048)],
            writes=[("kvsel", 0, 2048)])

    S.serial = False
    xgy_ybf_w = [None] * NSB
    for isb in range(NSB):
        S.op("sy", lambda s, isb=isb: s.dma_start(
            out=a1(B_outsb).rearrange("p (c t) -> p c t", t=SBW),
            in_=out_sp.rearrange("(c p) t -> p c t", p=P)[:, :, isb * SBW:(isb + 1) * SBW]),
            inc=16, sem="doutsb", reads=[("out_sp", 0, 1)], writes=[fa1(B_outsb)])
        S.op("sy", lambda s, isb=isb: s.dma_start(
            out=a2(B_go2b).rearrange("p (c t) -> p c t", t=SBW),
            in_=qx_sp.rearrange("(c p) t -> p c t", p=P)[:, :, isb * SBW:(isb + 1) * SBW]),
            inc=16, sem="dqxsb", reads=[("qx_sp", 0, 1)], writes=[fa2(B_go2b)])
        S.op("sc", lambda s, isb=isb: s.dma_start(
            out=ysb.rearrange("p (c t) -> p c t", t=SBW),
            in_=xT.rearrange("(c p) t -> p c t", p=P)[:, :, isb * SBW:(isb + 1) * SBW]),
            inc=16, sem="dysb", writes=[("ysb", 0, 8192)])
        for oc in range(NCK):
            slw = oc % 3
            par = oc % 2
            S.op("gp", lambda g, oc=oc, slw=slw: g.dma_start(
                out=ws[:, slw * 2048:(slw + 1) * 2048],
                in_=gatet[oc * P:(oc + 1) * P, :]), inc=16, sem=f"dws{slw}",
                writes=[("ws", slw * 2048, (slw + 1) * 2048)])

            def gmm(t, slw=slw, isb=isb, par=par):
                ins = None
                for fc in range(NCK):
                    ins = t.matmul(
                        pMM[par][:, :],
                        ws[:, slw * 2048 + fc * 128:slw * 2048 + (fc + 1) * 128],
                        xgy[:, fc * T + isb * SBW: fc * T + (isb + 1) * SBW],
                        start=(fc == 0), stop=(fc == NCK - 1))
                return ins
            S.op("te", gmm,
                 reads=[("ws", slw * 2048, (slw + 1) * 2048), XGY_ALL],
                 writes=[(f"pMM{par}", 0, 512)])
            S.op("sc", lambda s, oc=oc, par=par: s.activation(
                a2(B_gateb, oc * SBW, SBW), pMM[par][:, :], AF.Sigmoid),
                reads=[(f"pMM{par}", 0, 512)], writes=[fa2(B_gateb, oc * SBW, SBW)])

        for oc in range(NCK):
            par = oc % 2
            S.op("te", lambda t, oc=oc, par=par: t.matmul(
                pMM[par][:, :], kvsel[:, oc * 128:(oc + 1) * 128],
                a2(B_go2b, oc * SBW, SBW), start=True, stop=True),
                reads=[("kvsel", oc * 128, (oc + 1) * 128),
                       fa2(B_go2b, oc * SBW, SBW)],
                writes=[(f"pMM{par}", 0, 512)])
            S.op("ve", lambda v, oc=oc, par=par: v.tensor_tensor(
                a1(B_outsb, oc * SBW, SBW), a1(B_outsb, oc * SBW, SBW),
                pMM[par][:, :], ALU.add),
                reads=[fa1(B_outsb, oc * SBW, SBW), (f"pMM{par}", 0, 512)],
                writes=[fa1(B_outsb, oc * SBW, SBW)])

        for oc in range(NCK):
            par = oc % 2
            S.op("sc", lambda s, oc=oc, par=par: s.activation(
                outsq[:, par * 512:(par + 1) * 512], a1(B_outsb, oc * SBW, SBW),
                AF.Square),
                reads=[fa1(B_outsb, oc * SBW, SBW)],
                writes=[("outsq", par * 512, (par + 1) * 512)])
            S.op("te", lambda t, oc=oc, par=par: t.matmul(
                pQK[0][0:1, :], onesb[:, :], outsq[:, par * 512:(par + 1) * 512],
                start=(oc == 0), stop=(oc == NCK - 1)),
                reads=[("onesb", 0, 1), ("outsq", par * 512, (par + 1) * 512)],
                writes=[("pQK0", 0, 512)])
        S.op("ve", lambda v: v.tensor_scalar(
            rowsA, pQK[0][0:1, :], 1.0 / 2048.0, EPS_RMS, ALU.mult, ALU.add),
            reads=[("pQK0", 0, 512)], writes=[fsw(R_A)])
        S.op("sc", lambda s: s.activation(rowsB, rowsA, AF.Ln),
             reads=[fsw(R_A)], writes=[fsw(R_B)])
        S.op("sc", lambda s: s.activation(rowsB, rowsB, AF.Exp, scale=-0.5),
             reads=[fsw(R_B)], writes=[fsw(R_B)])
        S.op("te", lambda t: t.matmul(pQK[1][:, :], onesK[0:1, :], rowsB,
                                      start=True, stop=True),
             reads=[("onesK", 0, 128), fsw(R_B)], writes=[("pQK1", 0, 512)])
        for oc in range(NCK):
            par = oc % 2
            S.op("ve", lambda v, oc=oc, par=par: v.scalar_tensor_tensor(
                go1[:, par * 512:(par + 1) * 512], a2(B_gateb, oc * SBW, SBW),
                nwsb[:, oc:oc + 1], a1(B_outsb, oc * SBW, SBW), ALU.mult, ALU.mult),
                reads=[fa2(B_gateb, oc * SBW, SBW), ("nwsb", oc, oc + 1),
                       fa1(B_outsb, oc * SBW, SBW)],
                writes=[("go1", par * 512, (par + 1) * 512)])
            S.op("ve", lambda v, oc=oc, par=par: v.tensor_tensor(
                a2(B_go2b, oc * SBW, SBW), go1[:, par * 512:(par + 1) * 512],
                pQK[1][:, :], ALU.mult),
                reads=[("go1", par * 512, (par + 1) * 512), ("pQK1", 0, 512)],
                writes=[fa2(B_go2b, oc * SBW, SBW)])

        for gc in range(NCK):
            slw = gc % 3
            par = gc % 2
            S.op("gp", lambda g, gc=gc, slw=slw: g.dma_start(
                out=ws[:, slw * 2048:(slw + 1) * 2048],
                in_=woutt[gc * P:(gc + 1) * P, :]), inc=16, sem=f"dws{slw}",
                writes=[("ws", slw * 2048, (slw + 1) * 2048)])

            def wmm(t, slw=slw, par=par):
                ins = None
                for oc in range(NCK):
                    ins = t.matmul(
                        pMM[par][:, :],
                        ws[:, slw * 2048 + oc * 128:slw * 2048 + (oc + 1) * 128],
                        a2(B_go2b, oc * SBW, SBW),
                        start=(oc == 0), stop=(oc == NCK - 1))
                return ins
            S.op("te", wmm,
                 reads=[("ws", slw * 2048, (slw + 1) * 2048), fa2(B_go2b)],
                 writes=[(f"pMM{par}", 0, 512)])
            S.op("ve", lambda v, gc=gc, par=par: v.scalar_tensor_tensor(
                ysb[:, gc * SBW:(gc + 1) * SBW], pMM[par][:, :], 0.0,
                ysb[:, gc * SBW:(gc + 1) * SBW], ALU.add, ALU.add),
                reads=[(f"pMM{par}", 0, 512), ("ysb", gc * SBW, (gc + 1) * SBW)],
                writes=[("ysb", gc * SBW, (gc + 1) * SBW)])

        def ln_stats_norm(store_fn):
            for gc in range(NCK):
                par = gc % 2
                S.op("ve", lambda s, gc=gc, par=par: s.tensor_copy(
                    lnb[:, par * 512:(par + 1) * 512], ysb[:, gc * SBW:(gc + 1) * SBW]),
                    reads=[("ysb", gc * SBW, (gc + 1) * SBW)],
                    writes=[("lnb", par * 512, (par + 1) * 512)])
                S.op("te", lambda t, gc=gc, par=par: t.matmul(
                    pO[0][0:1, :], onesb[:, :], lnb[:, par * 512:(par + 1) * 512],
                    start=(gc == 0), stop=(gc == NCK - 1)),
                    reads=[("onesb", 0, 1), ("lnb", par * 512, (par + 1) * 512)],
                    writes=[("pO0", 0, 512)])
                S.op("sc", lambda s, gc=gc, par=par: s.activation(
                    sqb[:, par * 512:(par + 1) * 512], ysb[:, gc * SBW:(gc + 1) * SBW],
                    AF.Square),
                    reads=[("ysb", gc * SBW, (gc + 1) * SBW)],
                    writes=[("sqb", par * 512, (par + 1) * 512)])
                S.op("te", lambda t, gc=gc, par=par: t.matmul(
                    pO[1][0:1, :], onesb[:, :], sqb[:, par * 512:(par + 1) * 512],
                    start=(gc == 0), stop=(gc == NCK - 1)),
                    reads=[("onesb", 0, 1), ("sqb", par * 512, (par + 1) * 512)],
                    writes=[("pO1", 0, 512)])
            S.op("ve", lambda v: v.tensor_scalar(
                rowsC, pO[0][0:1, :], 1.0 / 2048.0, None, ALU.mult),
                reads=[("pO0", 0, 512)], writes=[fsw(R_C)])
            S.op("ve", lambda v: v.tensor_scalar(
                rowsA, pO[1][0:1, :], 1.0 / 2048.0, None, ALU.mult),
                reads=[("pO1", 0, 512)], writes=[fsw(R_A)])
            S.op("ve", lambda v: v.tensor_tensor(rowsD, rowsC, rowsC, ALU.mult),
                 reads=[fsw(R_C)], writes=[fsw(R_D)])
            S.op("ve", lambda v: v.tensor_sub(rowsA, rowsA, rowsD),
                 reads=[fsw(R_A), fsw(R_D)], writes=[fsw(R_A)])
            S.op("ve", lambda v: v.tensor_scalar(
                rowsA, rowsA, 1.0, EPS_LN, ALU.mult, ALU.add),
                reads=[fsw(R_A)], writes=[fsw(R_A)])
            S.op("sc", lambda s: s.activation(rowsB, rowsA, AF.Ln),
                 reads=[fsw(R_A)], writes=[fsw(R_B)])
            S.op("sc", lambda s: s.activation(rowsB, rowsB, AF.Exp, scale=-0.5),
                 reads=[fsw(R_B)], writes=[fsw(R_B)])
            S.op("ve", lambda v: v.tensor_scalar(rowsC, rowsC, -1.0, None, ALU.mult),
                 reads=[fsw(R_C)], writes=[fsw(R_C)])
            S.op("te", lambda t: t.matmul(pO[0][:, :], onesK[0:1, :], rowsC,
                                          start=True, stop=True),
                 reads=[("onesK", 0, 128), fsw(R_C)], writes=[("pO0", 0, 512)])
            S.op("te", lambda t: t.matmul(pO[1][:, :], onesK[0:1, :], rowsB,
                                          start=True, stop=True),
                 reads=[("onesK", 0, 128), fsw(R_B)], writes=[("pO1", 0, 512)])
            for gc in range(NCK):
                S.op("ve", lambda v, gc=gc: v.tensor_tensor(
                    ysb[:, gc * SBW:(gc + 1) * SBW], ysb[:, gc * SBW:(gc + 1) * SBW],
                    pO[0][:, :], ALU.add),
                    reads=[("ysb", gc * SBW, (gc + 1) * SBW), ("pO0", 0, 512)],
                    writes=[("ysb", gc * SBW, (gc + 1) * SBW)])
                S.op("ve", lambda v, gc=gc: v.tensor_tensor(
                    ysb[:, gc * SBW:(gc + 1) * SBW], ysb[:, gc * SBW:(gc + 1) * SBW],
                    pO[1][:, :], ALU.mult),
                    reads=[("ysb", gc * SBW, (gc + 1) * SBW), ("pO1", 0, 512)],
                    writes=[("ysb", gc * SBW, (gc + 1) * SBW)])
                store_fn(gc)

        def b_store(gc, isb=isb):
            xgy_ybf_w[isb] = S.op("ve", lambda v, gc=gc, isb=isb: v.tensor_copy(
                xgy[:, gc * T + isb * SBW: gc * T + (isb + 1) * SBW],
                ysb[:, gc * SBW:(gc + 1) * SBW]),
                reads=[("ysb", gc * SBW, (gc + 1) * SBW)],
                writes=[("xgy", gc * T + isb * SBW, gc * T + (isb + 1) * SBW)])
        ln_stats_norm(b_store)
        S.op("sc", lambda s, isb=isb: s.dma_start(
            out=y_sp.rearrange("(c p) t -> p c t", p=P)[:, :, isb * SBW:(isb + 1) * SBW],
            in_=ysb.rearrange("p (c t) -> p c t", t=SBW)), inc=16, sem="dywr",
            reads=[("ysb", 0, 8192)],
            writes=[("y_sp", isb * SBW, (isb + 1) * SBW)])

        for e in range(8):
            ebank = f"pQK{e % 2}"
            on_ve = True
            for half2 in range(8):
                fc0 = half2 * 2
                hf, q4 = fc0 // 8, (fc0 % 8) // 2
                slot = half2 % 2
                row0 = (e * 8 + isb * 2 + hf) * P
                qeng = "sy" if half2 % 2 == 0 else "sc"
                S.op(qeng, lambda g, slot=slot, row0=row0, q4=q4: g.dma_start(
                    out=swbuf[:, slot * 1024:(slot + 1) * 1024],
                    in_=swt[row0:row0 + P, q4 * 1024:(q4 + 1) * 1024]),
                    inc=16, sem=f"dsw{slot}",
                    writes=[("swbuf", slot * 1024, (slot + 1) * 1024)])
                for i2 in range(2):
                    fc = fc0 + i2
                    S.op("ve", lambda v, fc=fc, slot=slot, i2=i2: v.tensor_tensor(
                        prodf[:, (slot % 2) * 512:(slot % 2 + 1) * 512],
                        ysb[:, fc * 512:(fc + 1) * 512],
                        swbuf[:, slot * 1024 + i2 * 512:slot * 1024 + (i2 + 1) * 512],
                        ALU.mult),
                        reads=[("ysb", fc * 512, (fc + 1) * 512),
                               ("swbuf", slot * 1024 + i2 * 512,
                                slot * 1024 + (i2 + 1) * 512)],
                        writes=[("prodf", (slot % 2) * 512, (slot % 2 + 1) * 512)])
                    if on_ve:
                        if fc == 0:
                            S.op("ve", lambda v, slot=slot, e=e: v.tensor_copy(
                                pQK[e % 2][:, :],
                                prodf[:, (slot % 2) * 512:(slot % 2 + 1) * 512]),
                                reads=[("prodf", (slot % 2) * 512,
                                        (slot % 2 + 1) * 512)],
                                writes=[(ebank, 0, 512)])
                        else:
                            S.op("ve", lambda v, slot=slot, e=e: v.tensor_tensor(
                                pQK[e % 2][:, :], pQK[e % 2][:, :],
                                prodf[:, (slot % 2) * 512:(slot % 2 + 1) * 512],
                                ALU.add),
                                reads=[(ebank, 0, 512),
                                       ("prodf", (slot % 2) * 512,
                                        (slot % 2 + 1) * 512)],
                                writes=[(ebank, 0, 512)])
                    else:
                        S.op("te", lambda t, fc=fc, slot=slot, e=e: t.matmul(
                            pQK[e % 2][0:1, :], onesM[:, :],
                            prodf[:, (slot % 2) * 512:(slot % 2 + 1) * 512],
                            start=(fc == 0), stop=(fc == NCK - 1)),
                            reads=[("onesM", 0, 1),
                                   ("prodf", (slot % 2) * 512, (slot % 2 + 1) * 512)],
                            writes=[(ebank, 0, 512)])
            if on_ve:
                S.op("sc", lambda s, e=e: s.copy(
                    prodf[:, 0:512], pQK[e % 2][:, :]),
                    reads=[(ebank, 0, 512)], writes=[("prodf", 0, 512)])
                S.op("te", lambda t, e=e: t.matmul(
                    pQK[e % 2][0:1, :], onesM[:, :], prodf[:, 0:512],
                    start=True, stop=True),
                    reads=[("onesM", 0, 1), ("prodf", 0, 512)],
                    writes=[(ebank, 0, 512)])
            col = e * 4 + isb
            S.op("ve", lambda v, e=e, col=col: v.reduce_sum(
                accsc[0:1, col:col + 1], pQK[e % 2][0:1, :], axis=AX.X),
                reads=[(f"pQK{e % 2}", 0, 512)], writes=[("accsc", col, col + 1)])

    S.serial = False
    for e in range(8):
        S.op("ve", lambda v, e=e: v.reduce_sum(
            lgrow[:, e:e + 1], accsc[0:1, e * 4:(e + 1) * 4], axis=AX.X),
            reads=[("accsc", e * 4, (e + 1) * 4)], writes=[("lgrow", e, e + 1)])
    S.op("ve", lambda v: v.tensor_add(lgrow[:, :], lgrow[:, :], swbs[:, :]),
         reads=[("lgrow", 0, 8), ("swbs", 0, 8)], writes=[("lgrow", 0, 8)])
    S.op("ve", lambda v: v.reduce_max(m1r[:, :], lgrow[:, :], axis=AX.X),
         reads=[("lgrow", 0, 8)], writes=[("m1r", 0, 1)])
    S.op("ve", lambda v: v.tensor_scalar(m1r[:, :], m1r[:, :], -1.0, None, ALU.mult),
         reads=[("m1r", 0, 1)], writes=[("m1r", 0, 1)])
    S.op("sc", lambda s: s.activation(exrow[:, :], lgrow[:, :], AF.Exp,
                                      bias=m1r[0:1, 0:1]),
         reads=[("lgrow", 0, 8), ("m1r", 0, 1)], writes=[("exrow", 0, 8)])
    S.op("ve", lambda v: v.reduce_sum(srow[:, :], exrow[:, :], axis=AX.X),
         reads=[("exrow", 0, 8)], writes=[("srow", 0, 1)])
    S.op("ve", lambda v: v.reciprocal(srow[:, :], srow[:, :]),
         reads=[("srow", 0, 1)], writes=[("srow", 0, 1)])
    S.op("ve", lambda v: v.tensor_scalar(probs[:, :], exrow[:, :], srow[0:1, 0:1],
                                         None, ALU.mult),
         reads=[("exrow", 0, 8), ("srow", 0, 1)], writes=[("probs", 0, 8)])
    S.op("ve", lambda v: v.reduce_max(m1r[:, :], probs[:, :], axis=AX.X),
         reads=[("probs", 0, 8)], writes=[("m1r", 0, 1)])
    S.op("ve", lambda v: v.tensor_scalar(msk1[:, :], probs[:, :], m1r[0:1, 0:1],
                                         None, ALU.is_ge),
         reads=[("probs", 0, 8), ("m1r", 0, 1)], writes=[("msk1", 0, 8)])
    S.op("ve", lambda v: v.tensor_mul(tmp8[:, :], probs[:, :], msk1[:, :]),
         reads=[("probs", 0, 8), ("msk1", 0, 8)], writes=[("tmp8", 0, 8)])
    S.op("ve", lambda v: v.tensor_sub(tmp8[:, :], probs[:, :], tmp8[:, :]),
         reads=[("probs", 0, 8), ("tmp8", 0, 8)], writes=[("tmp8", 0, 8)])
    S.op("ve", lambda v: v.reduce_max(m2r[:, :], tmp8[:, :], axis=AX.X),
         reads=[("tmp8", 0, 8)], writes=[("m2r", 0, 1)])
    S.op("ve", lambda v: v.tensor_scalar(msk2[:, :], tmp8[:, :], m2r[0:1, 0:1],
                                         None, ALU.is_ge),
         reads=[("tmp8", 0, 8), ("m2r", 0, 1)], writes=[("msk2", 0, 8)])
    S.op("ve", lambda v: v.tensor_mul(exrow[:, :], msk2[:, :], msk1[:, :]),
         reads=[("msk2", 0, 8), ("msk1", 0, 8)], writes=[("exrow", 0, 8)])
    S.op("ve", lambda v: v.tensor_sub(msk2[:, :], msk2[:, :], exrow[:, :]),
         reads=[("msk2", 0, 8), ("exrow", 0, 8)], writes=[("msk2", 0, 8)])
    S.op("ve", lambda v: v.tensor_mul(pkrow[:, 16:24], msk1[:, :], probs[:, :]),
         reads=[("msk1", 0, 8), ("probs", 0, 8)], writes=[("swbuf", 16, 24)])
    S.op("ve", lambda v: v.tensor_mul(pkrow[:, 24:32], msk2[:, :], probs[:, :]),
         reads=[("msk2", 0, 8), ("probs", 0, 8)], writes=[("swbuf", 24, 32)])
    S.op("ve", lambda v: (
        v.tensor_copy(pkrow[:, 0:8], msk1[:, :]),
        v.tensor_copy(pkrow[:, 8:16], msk2[:, :]),
        v.memset(pkrow[:, 32:64], 0.0),
    )[-1], reads=[("msk1", 0, 8), ("msk2", 0, 8)],
        writes=[("swbuf", 0, 16), ("swbuf", 32, 64)])
    for s4 in range(4):
        S.op("ve", lambda v, s4=s4: v.tensor_scalar(
            sendm, pkrow, ohss[0:1, s4:s4 + 1], None, ALU.mult),
            reads=[fsw(PKR), ("ohss", s4, s4 + 1)], writes=[fsw(SDM)])
        S.op("gp", lambda g, s4=s4: g.dma_start(
            out=cc_mk_i[s4:s4 + 1, :], in_=sendm), inc=16, sem="dmki",
            reads=[fsw(SDM)], writes=[("cc_mk_i", s4, s4 + 1)])
    t_cc_mk = S.op("gp", lambda g: g.collective_compute(
        "AllReduce", ALU.add, replica_groups=[list(range(8))],
        ins=[cc_mk_i], outs=[cc_mk_o]), coll="coll_mk",
        reads=[("cc_mk_i", 0, 4)], writes=[("cc_mk_o", 0, 1)])
    t_cc_b2 = S.op("gp", lambda g: g.collective_compute(
        "AllReduce", ALU.add, replica_groups=[list(range(8))],
        ins=[cc_mk_o], outs=[cc_b2_o]), coll="coll_b2",
        reads=[("cc_mk_o", 0, 1)], writes=[("cc_b2_o", 0, 1)])
    _prev_rmv = [None]
    for s4 in range(4):
        _prev_rmv[0] = S.op("sc", lambda g, s4=s4: g.dma_start(
            out=rmv[:, s4 * 64:(s4 + 1) * 64], in_=cc_mk_o[s4:s4 + 1, :]), inc=16,
            sem="drmv", waits=[t_cc_b2, _prev_rmv[0]],
            reads=[("cc_mk_o", 0, 1)],
            writes=[("swbuf", 128 + s4 * 64, 128 + (s4 + 1) * 64)])
    S.op("ve", lambda v: v.memset(selrow, 0.0), writes=[fsw(SELR)])
    for s4 in range(4):
        S.op("ve", lambda v, s4=s4: v.scalar_tensor_tensor(
            selrow, rmv[:, s4 * 64:(s4 + 1) * 64], ohas[0:1, s4:s4 + 1],
            selrow, ALU.mult, ALU.add),
            reads=[("swbuf", 128 + s4 * 64, 128 + (s4 + 1) * 64),
                   ("ohas", s4, s4 + 1), fsw(SELR)],
            writes=[fsw(SELR)])
    S.op("te", lambda t: t.matmul(pMM[1][:, 0:64], onesK[0:1, :], selrow[0:1, :],
                                  start=True, stop=True),
         reads=[("onesK", 0, 128), fsw(SELR)], writes=[("pMM1", 0, 512)])
    S.op("ve", lambda v: v.tensor_copy(rb[:, :], pMM[1][:, 0:64]),
         reads=[("pMM1", 0, 512)], writes=[("rb", 0, 64)])

    def accum_sel(dst, dkey, src_ap_fn, skey_fn, coff):
        S.op("ve", lambda v: v.tensor_scalar(
            dst[:, :], src_ap_fn(0), rb[:, coff:coff + 1], None, ALU.mult),
            reads=[skey_fn(0), ("rb", coff, coff + 1)], writes=[dkey])
        for e in range(1, 8):
            S.op("ve", lambda v, e=e: v.scalar_tensor_tensor(
                dst[:, :], src_ap_fn(e), rb[:, coff + e:coff + e + 1], dst[:, :],
                ALU.mult, ALU.add),
                reads=[skey_fn(e), ("rb", coff + e, coff + e + 1), dkey],
                writes=[dkey])

    accum_sel(w1s1, ("w1s1", 0, 128),
              lambda e: w1b[:, e * 128:(e + 1) * 128],
              lambda e: ("w1b", e * 128, (e + 1) * 128), 0)
    accum_sel(w1s2, ("w1s2", 0, 128),
              lambda e: w1b[:, e * 128:(e + 1) * 128],
              lambda e: ("w1b", e * 128, (e + 1) * 128), 8)
    accum_sel(w2s1, ("w2s1", 0, 128),
              lambda e: w2b[:, e * 128:(e + 1) * 128],
              lambda e: ("w2b", e * 128, (e + 1) * 128), 16)
    accum_sel(w2s2, ("w2s2", 0, 128),
              lambda e: w2b[:, e * 128:(e + 1) * 128],
              lambda e: ("w2b", e * 128, (e + 1) * 128), 24)
    accum_sel(b1s1, ("b1s1", 0, 1), lambda e: b1sb[:, e:e + 1],
              lambda e: ("b1sb", e, e + 1), 0)
    accum_sel(b1s2, ("b1s2", 0, 1), lambda e: b1sb[:, e:e + 1],
              lambda e: ("b1sb", e, e + 1), 8)
    accum_sel(b2c, ("b2c", 0, 1), lambda e: b2sb[:, e:e + 1],
              lambda e: ("b2sb", e, e + 1), 16)
    for e in range(8):
        S.op("ve", lambda v, e=e: v.scalar_tensor_tensor(
            b2c[:, :], b2sb[:, e:e + 1], rb[:, 24 + e:25 + e], b2c[:, :],
            ALU.mult, ALU.add),
            reads=[("b2sb", e, e + 1), ("rb", 24 + e, 25 + e), ("b2c", 0, 1)],
            writes=[("b2c", 0, 1)])

    S.serial = False
    for isb in range(NSB):
        S.op("sy", lambda s, isb=isb: s.dma_start(
            out=ysb.rearrange("p (c t) -> p c t", t=SBW),
            in_=y_sp.rearrange("(c p) t -> p c t", p=P)[:, :, isb * SBW:(isb + 1) * SBW]),
            inc=16, sem="dysb", reads=[("y_sp", isb * SBW, (isb + 1) * SBW)],
            writes=[("ysb", 0, 8192)])
        HB = [pO[0], pO[1], pQK[1]]
        HK = ["pO0", "pO1", "pQK1"]
        for dc in range(NCK):
            b1x = (2 * dc) % 3
            b2x = (2 * dc + 1) % 3
            S.op("te", lambda t, dc=dc, isb=isb, b1x=b1x: t.matmul(
                HB[b1x][:, :], w1s1[:, :],
                xgy[:, dc * T + isb * SBW: dc * T + (isb + 1) * SBW],
                start=True, stop=True),
                reads=[("w1s1", 0, 128), XGY_ALL], writes=[(HK[b1x], 0, 512)])
            S.op("te", lambda t, dc=dc, isb=isb, b2x=b2x: t.matmul(
                HB[b2x][:, :], w1s2[:, :],
                xgy[:, dc * T + isb * SBW: dc * T + (isb + 1) * SBW],
                start=True, stop=True),
                reads=[("w1s2", 0, 128), XGY_ALL], writes=[(HK[b2x], 0, 512)])
            S.op("sc", lambda s, b1x=b1x: s.activation(h1b, HB[b1x][:, :], AF.Gelu,
                                                       bias=b1s1[:, :]),
                 reads=[(HK[b1x], 0, 512), ("b1s1", 0, 1)], writes=[("go1", 0, 512)])
            S.op("sc", lambda s, b2x=b2x: s.activation(h2b, HB[b2x][:, :], AF.Gelu,
                                                       bias=b1s2[:, :]),
                 reads=[(HK[b2x], 0, 512), ("b1s2", 0, 1)], writes=[("go1", 512, 1024)])
            S.op("te", lambda t: t.matmul(pQK[0][:, :], w2s1[:, :], h1b,
                                          start=True, stop=False),
                 reads=[("w2s1", 0, 128), ("go1", 0, 512)], writes=[("pQK0", 0, 512)])
            S.op("te", lambda t: t.matmul(pQK[0][:, :], w2s2[:, :], h2b,
                                          start=False, stop=True),
                 reads=[("w2s2", 0, 128), ("go1", 512, 1024)],
                 writes=[("pQK0", 0, 512)])
            S.op("ve", lambda v, dc=dc: v.scalar_tensor_tensor(
                a1(B_outsb, dc * SBW, SBW), pQK[0][:, :], b2c[:, :],
                ysb[:, dc * SBW:(dc + 1) * SBW], ALU.add, ALU.add),
                reads=[("pQK0", 0, 512), ("b2c", 0, 1),
                       ("ysb", dc * SBW, (dc + 1) * SBW)],
                writes=[fa1(B_outsb, dc * SBW, SBW)])
        for gc in range(NCK):
            slw = gc % 3
            par = gc % 2
            S.op("gp", lambda g, gc=gc, slw=slw: g.dma_start(
                out=ws[:, slw * 2048:(slw + 1) * 2048],
                in_=w3t[gc * P:(gc + 1) * P, :]), inc=16, sem=f"dws{slw}",
                writes=[("ws", slw * 2048, (slw + 1) * 2048)])

            def w3mm(t, slw=slw, par=par):
                ins = None
                for dc in range(NCK):
                    ins = t.matmul(
                        pMM[par][:, :],
                        ws[:, slw * 2048 + dc * 128:slw * 2048 + (dc + 1) * 128],
                        a1(B_outsb, dc * SBW, SBW),
                        start=(dc == 0), stop=(dc == NCK - 1))
                return ins
            S.op("te", w3mm,
                 reads=[("ws", slw * 2048, (slw + 1) * 2048), fa1(B_outsb)],
                 writes=[(f"pMM{par}", 0, 512)])
            S.op("ve", lambda v, gc=gc, par=par: v.scalar_tensor_tensor(
                ysb[:, gc * SBW:(gc + 1) * SBW], pMM[par][:, :], b3sb[:, gc:gc + 1],
                ysb[:, gc * SBW:(gc + 1) * SBW], ALU.add, ALU.add),
                reads=[(f"pMM{par}", 0, 512), ("b3sb", gc, gc + 1),
                       ("ysb", gc * SBW, (gc + 1) * SBW)],
                writes=[("ysb", gc * SBW, (gc + 1) * SBW)])
        def d_store(gc, isb=isb):
            pass
        ln_stats_norm(d_store)
        S.op("sy", lambda s, isb=isb: s.dma_start(
            out=yTo.rearrange("(c p) t -> p c t", p=P)[:, :, isb * SBW:(isb + 1) * SBW],
            in_=ysb.rearrange("p (c t) -> p c t", t=SBW)), inc=16, sem="dyo",
            reads=[("ysb", 0, 8192)])

    final_waits = [("dyo", S.cnt.get("dyo", 0))]

    blk = es.enter_context(nc.Block())

    def emit(eng_name):
        def body(e):
            for fn, ws_, inc, coll, semname in S.ops[eng_name]:
                for (weng, val) in ws_:
                    e.wait_ge(sems[weng], val)
                ins = fn(e)
                ins.then_inc(sems[semname], 1 if coll is not None else inc)
            if eng_name == "sy":
                for sname, val in final_waits:
                    e.wait_ge(sems[sname], val)
        return body

    blk.sync(emit("sy"))
    blk.gpsimd(emit("gp"))
    blk.tensor(emit("te"))
    blk.scalar(emit("sc"))
    blk.vector(emit("ve"))

    es.close()
    return nc


_NC_CACHE = None


def _get_nc():
    global _NC_CACHE
    if _NC_CACHE is None:
        _NC_CACHE = build_nc()
    return _NC_CACHE


def kernel(L0_attn_hidden_states, L0_attn_slope_rate, L0_attn_w_qkv, L0_attn_w_gate,
           L0_attn_w_out, L0_attn_norm_w, L0_ffn_switch_w, L0_ffn_switch_b,
           L0_ffn_w1, L0_ffn_b1, L0_ffn_w2, L0_ffn_b2, L0_ffn_w3, L0_ffn_b3):
    x = np.ascontiguousarray(np.asarray(L0_attn_hidden_states, np.float32))
    slopes = np.asarray(L0_attn_slope_rate, np.float32).reshape(1, 16)
    wqkv = np.asarray(L0_attn_w_qkv, np.float32)
    wgate = np.asarray(L0_attn_w_gate, np.float32)
    wout = np.asarray(L0_attn_w_out, np.float32)
    nw = np.asarray(L0_attn_norm_w, np.float32)
    sw = np.asarray(L0_ffn_switch_w, np.float32)
    swb = np.asarray(L0_ffn_switch_b, np.float32).reshape(1, 8)
    w1 = np.asarray(L0_ffn_w1, np.float32)
    b1 = np.asarray(L0_ffn_b1, np.float32)
    w2 = np.asarray(L0_ffn_w2, np.float32)
    b2 = np.asarray(L0_ffn_b2, np.float32)
    w3 = np.asarray(L0_ffn_w3, np.float32)
    b3 = np.asarray(L0_ffn_b3, np.float32)

    def slab(wT, noc):
        t = wT.reshape(16, 128, noc, 128).transpose(2, 1, 0, 3)
        return np.ascontiguousarray(t.reshape(noc * 128, 16 * 128))

    qkvt = slab(wqkv.T, 48)
    gatet = slab(wgate.T, 16)
    woutt = np.ascontiguousarray(
        wout.T.reshape(16, 128, 16, 128).transpose(2, 1, 0, 3).reshape(2048, 2048))
    w3t = np.ascontiguousarray(
        w3.T.reshape(16, 128, 16, 128).transpose(2, 1, 0, 3).reshape(2048, 2048))
    swf = sw.reshape(8, 2048, 16, 128).transpose(0, 2, 3, 1)
    swf = swf.reshape(8, 2, 8, 128, 4, 512).transpose(0, 4, 1, 3, 2, 5)
    swt = np.ascontiguousarray(swf.reshape(8 * 4 * 2 * 128, 8 * 512))
    w1t = np.ascontiguousarray(w1.transpose(0, 2, 1).reshape(8 * 128, 128))
    w2t = np.ascontiguousarray(w2.transpose(0, 2, 1).reshape(8 * 128, 128))
    b1t = np.ascontiguousarray(b1.T)
    b2t = np.ascontiguousarray(b2.T)
    b3fm = np.ascontiguousarray(b3.reshape(16, 128).T)
    nwfm = np.ascontiguousarray(nw.reshape(16, 128).T)

    shared = dict(qkvt=qkvt, gatet=gatet, woutt=woutt, w3t=w3t, swt=swt,
                  w1t=w1t, w2t=w2t, b1t=b1t, b2t=b2t, b3fm=b3fm, nwfm=nwfm,
                  swbr=swb, slr=slopes)
    in_maps = []
    for c in range(8):
        b_, half = c // 2, c % 2
        xc = x[b_, half * 2048:(half + 1) * 2048, :]
        xTc = np.ascontiguousarray(xc.T)
        ohs_ = np.zeros((1, 4), np.float32)
        ohr_ = np.zeros((1, 4), np.float32)
        oha_ = np.zeros((1, 4), np.float32)
        oha_[0, b_] = 1.0
        if half == 0:
            ohs_[0, b_] = 1.0
        else:
            ohr_[0, b_] = 1.0
        m = dict(shared)
        m.update(xT=xTc, ohs=ohs_, ohr=ohr_, oha=oha_)
        in_maps.append(m)

    nc = _get_nc()
    res = run_bass_kernel_spmd(nc, in_maps, core_ids=list(range(8)))
    out = np.empty((4, 4096, 2048), np.float32)
    for c in range(8):
        b_, half = c // 2, c % 2
        out[b_, half * 2048:(half + 1) * 2048, :] = res.results[c]["yTo"].T
    return out
